# version 04
# baseline (speedup 1.0000x reference)
import numpy as np
import ml_dtypes
from contextlib import ExitStack

import concourse.bass as bass
import concourse.mybir as mybir
from concourse.bass_utils import run_bass_kernel_spmd

F32 = mybir.dt.float32
BF16 = mybir.dt.bfloat16
I32 = mybir.dt.int32
AF = mybir.ActivationFunctionType
ALU = mybir.AluOpType
AX = mybir.AxisListType
bfloat16 = ml_dtypes.bfloat16

D = 4096
L = 4096
OWN = 1024
NH = 16
INTER = 11008
NT_I = 88
EPS = 1e-5


def _ga_tables():
    g = [1.0, 1.0, 1.0]
    n = 8
    masks = sorted(range(n), key=lambda m: (bin(m).count("1"), m))
    inv = {m: i for i, m in enumerate(masks)}
    C = np.zeros((n, n, n), dtype=np.float32)
    for i, a in enumerate(masks):
        for j, b in enumerate(masks):
            sw, at = 0, a >> 1
            while at:
                sw += bin(at & b).count("1")
                at >>= 1
            s = -1.0 if sw & 1 else 1.0
            for t in range(3):
                if a >> t & 1 and b >> t & 1:
                    s *= g[t]
            C[i, j, inv[a ^ b]] = s
    grades = np.array([bin(m).count("1") for m in masks])
    rev = np.where(grades * (grades - 1) // 2 % 2 == 1, -1.0, 1.0).astype(np.float32)
    met = np.ones(8, dtype=np.float32)
    return C[:, :, [4, 5, 6]], rev, met


def _akron_bases():
    biv, rev, met = _ga_tables()
    mats = [(1.0 / 16.0) * np.diag(rev * met)] + [np.diag(rev) @ biv[:, :, c] for c in range(3)]
    I32m = np.eye(32, dtype=np.float32)
    return np.stack([np.kron(I32m, m.astype(np.float32)) for m in mats])


class Rot:

    def __init__(self, bufs):
        self.bufs = bufs
        self.free_tok = [None] * len(bufs)
        self.i = -1

    @property
    def slot(self):
        return self.i % len(self.bufs)

    def next(self):
        self.i += 1
        s = self.i % len(self.bufs)
        return self.bufs[s], self.free_tok[s]

    def done(self, tok):
        self.free_tok[self.i % len(self.bufs)] = tok


class P:
    ENGS = ("sync", "tensor", "vector", "scalar", "gpsimd")

    def __init__(self, nc, sems):
        self.nc = nc
        self.q = {e: [] for e in self.ENGS}
        self.sems = sems
        self.cnt = {name: 0 for name in sems}

    def tok(self, sem):
        return (sem, self.cnt[sem])

    def wait(self, eng, token):
        if token is None:
            return
        if isinstance(token, list):
            for t in token:
                self.wait(eng, t)
            return
        sem, val = token
        if val <= 0:
            return
        s = self.sems[sem]
        self.q[eng].append(lambda e, s=s, v=val: e.wait_ge(s, v))

    def waits(self, eng, tokens):
        seen = {}
        flat = []
        for t in tokens:
            if isinstance(t, list):
                flat.extend(t)
            else:
                flat.append(t)
        for t in flat:
            if t is None:
                continue
            sem, val = t
            if val > seen.get(sem, -1):
                seen[sem] = val
        for sem, val in seen.items():
            self.wait(eng, (sem, val))

    def dma(self, eng, out, in_, sem):
        self.cnt[sem] += 16
        tokv = (sem, self.cnt[sem])
        s = self.sems[sem]
        self.q[eng].append(
            lambda e, o=out, i=in_, s=s: e.dma_start(out=o, in_=i).then_inc(s, 16)
        )
        return tokv

    def op(self, eng, build, inc=None):
        if inc is None:
            self.q[eng].append(build)
            return None
        self.cnt[inc] += 1
        tokv = (inc, self.cnt[inc])
        s = self.sems[inc]
        self.q[eng].append(lambda e, b=build, s=s: b(e).then_inc(s, 1))
        return tokv

    def barrier(self, scratch_row, ones_src):
        dma_toks = [self.tok(n) for n in self.cnt
                    if n not in ("TE", "VE", "SE", "SY") and self.cnt[n] > 0]
        toks = {
            "tensor": self.tok("TE"),
            "vector": self.op("vector", lambda e: e.nop(), inc="VE"),
            "scalar": self.op("scalar", lambda e, d=scratch_row, s=ones_src: e.copy(d, s), inc="SE"),
            "sync": self.op("sync", lambda e: e.nop(), inc="SY"),
        }
        for eng in ("sync", "tensor", "vector", "scalar"):
            self.waits(eng, [t for o, t in toks.items() if o != eng] + dma_toks)


def build_program():
    nc = bass.Bass("TRN2", target_bir_lowering=False, dynamic_dma_scratch_size=2048)

    d_xT = nc.dram_tensor("xT", [D, L], F32, kind="ExternalInput")
    d_wqT = nc.dram_tensor("wqT", [D, D], F32, kind="ExternalInput")
    d_wkT = nc.dram_tensor("wkT", [D, D], F32, kind="ExternalInput")
    d_wvT = nc.dram_tensor("wvT", [D, D], F32, kind="ExternalInput")
    d_bq = nc.dram_tensor("bq", [128, 32], F32, kind="ExternalInput")
    d_bk = nc.dram_tensor("bk", [128, 32], F32, kind="ExternalInput")
    d_bv = nc.dram_tensor("bv", [1, D], BF16, kind="ExternalInput")
    d_bw = nc.dram_tensor("bw", [1, 48], F32, kind="ExternalInput")
    d_kbasis = nc.dram_tensor("kbasis", [128, 4 * 2 * 256], F32, kind="ExternalInput")
    d_onesrow = nc.dram_tensor("onesrow", [1, 128], F32, kind="ExternalInput")
    d_onesrow_bf = nc.dram_tensor("onesrow_bf", [1, 128], BF16, kind="ExternalInput")
    d_onescol_bf = nc.dram_tensor("onescol_bf", [128, 1], BF16, kind="ExternalInput")
    d_onescol_f = nc.dram_tensor("onescol_f", [128, 1], F32, kind="ExternalInput")
    d_exp8 = nc.dram_tensor("exp8", [8, 128], F32, kind="ExternalInput")
    d_gones = nc.dram_tensor("gones", [128, 32 * 32], BF16, kind="ExternalInput")
    d_giones = nc.dram_tensor("giones", [128, 11 * 86], BF16, kind="ExternalInput")
    d_qwg = nc.dram_tensor("qwg", [512, INTER], I32, kind="ExternalInput")
    d_qwu = nc.dram_tensor("qwu", [512, INTER], I32, kind="ExternalInput")
    d_qwd = nc.dram_tensor("qwd", [1376, D], I32, kind="ExternalInput")
    d_scg_t = nc.dram_tensor("scg_t", [8, 4 * INTER], F32, kind="ExternalInput")
    d_scu_t = nc.dram_tensor("scu_t", [8, 4 * INTER], F32, kind="ExternalInput")
    d_scd_t = nc.dram_tensor("scd_t", [8, 11 * D], F32, kind="ExternalInput")
    d_scg_p = nc.dram_tensor("scg_p", [32, NT_I * 128], F32, kind="ExternalInput")
    d_scu_p = nc.dram_tensor("scu_p", [32, NT_I * 128], F32, kind="ExternalInput")
    d_scd = nc.dram_tensor("scd", [86, D], F32, kind="ExternalInput")
    d_qzg = nc.dram_tensor("qzg", [32, 1376], I32, kind="ExternalInput")
    d_qzu = nc.dram_tensor("qzu", [32, 1376], I32, kind="ExternalInput")
    d_qzd = nc.dram_tensor("qzd", [86, 512], I32, kind="ExternalInput")
    d_out = nc.dram_tensor("out", [D, OWN], F32, kind="ExternalOutput")

    s_xln = nc.dram_tensor("s_xln", [D, L], BF16)
    s_kT = nc.dram_tensor("s_kT", [D, L], BF16)
    s_v = nc.dram_tensor("s_v", [L, D], BF16)
    s_qT = nc.dram_tensor("s_qT", [D, OWN], BF16)
    s_attn = nc.dram_tensor("s_attn", [D, OWN], F32)
    s_x2 = nc.dram_tensor("s_x2", [D, OWN], F32)
    s_xln2 = nc.dram_tensor("s_xln2", [D, OWN], BF16)
    s_hT = nc.dram_tensor("s_hT", [NT_I * 128, OWN], BF16)
    s_m2g = nc.dram_tensor("s_m2g", [32, NT_I * 128], F32)
    s_m2u = nc.dram_tensor("s_m2u", [32, NT_I * 128], F32)

    es = ExitStack()
    with es:
        sems = {}
        semnames = ["LD", "ST", "TE", "VE", "SE", "SY", "XL", "HL"]
        for t in ("A", "B", "C", "D", "E", "F", "G", "H", "I", "J", "M", "K", "V"):
            semnames += [t + "0", t + "1"]
        for name in semnames:
            sems[name] = es.enter_context(nc.semaphore(name))
        p = P(nc, sems)

        ones_bf = es.enter_context(nc.sbuf_tensor("ones_bf", [128, 1], BF16))
        ones_f = es.enter_context(nc.sbuf_tensor("ones_f", [128, 1], F32))
        onesrow = es.enter_context(nc.sbuf_tensor("sb_onesrow", [1, 128], F32))
        onesrow_bf = es.enter_context(nc.sbuf_tensor("sb_onesrow_bf", [1, 128], BF16))
        bq_sb = es.enter_context(nc.sbuf_tensor("bq_sb", [128, 32], F32))
        bk_sb = es.enter_context(nc.sbuf_tensor("bk_sb", [128, 32], F32))
        exp8 = es.enter_context(nc.sbuf_tensor("sb_exp8", [8, 128], F32))
        hg_sb = es.enter_context(nc.sbuf_tensor("hg_sb", [86, OWN], BF16))
        scrw = es.enter_context(nc.sbuf_tensor("scrw", [1, 4], F32))

        def barrier():
            p.barrier(scrw[:1, 0:1], ones_f[:1, 0:1])

        for dst, src in ((ones_bf, d_onescol_bf), (ones_f, d_onescol_f),
                         (onesrow, d_onesrow), (onesrow_bf, d_onesrow_bf),
                         (bq_sb, d_bq), (bk_sb, d_bk),
                         (exp8, d_exp8)):
            t_c = p.dma("sync", dst[:, :], src[:, :], "LD")

        p.waits("tensor", [t_c])
        barrier()

        CH = 256
        with ExitStack() as ph:
            xf = Rot([ph.enter_context(nc.sbuf_tensor(f"xf{i}", [128, 32 * CH], F32)) for i in range(2)])
            xbf = ph.enter_context(nc.sbuf_tensor("xbf", [128, 32 * CH], BF16))
            xsq = Rot([ph.enter_context(nc.sbuf_tensor(f"xsq{i}", [128, CH], BF16)) for i in range(2)])
            rows = ph.enter_context(nc.sbuf_tensor("rows", [1, 6 * CH], F32))
            rbs = ph.enter_context(nc.sbuf_tensor("rbs", [128, CH], F32))
            mbs = ph.enter_context(nc.sbuf_tensor("mbs", [128, CH], F32))
            tmpt = ph.enter_context(nc.sbuf_tensor("tmpt", [128, CH], F32))
            xlnb = ph.enter_context(nc.sbuf_tensor("xlnb", [128, 32 * CH], BF16))
            p_s1 = ph.enter_context(nc.psum_tensor("p_s1", [1, CH], F32))
            p_s2 = ph.enter_context(nc.psum_tensor("p_s2", [1, CH], F32))
            p_b1 = ph.enter_context(nc.psum_tensor("p_b1", [128, CH], F32))
            p_b2 = ph.enter_context(nc.psum_tensor("p_b2", [128, CH], F32))

            t_store = None
            t_pb_read = None
            t_ps_read = None
            for mc in range(16):
                col0 = mc * CH
                xfb, war_xf = xf.next()
                t_cast = []
                for kg in range(4):
                    p.waits("sync", [war_xf if kg == 0 else None])
                    ap = bass.AP(d_xT, (kg * 8 * 128) * L + col0,
                                 [[L, 128], [128 * L, 8], [1, CH]])
                    tl = p.dma("sync", xfb[:, kg * 8 * CH:(kg + 1) * 8 * CH], ap, "A%d" % xf.slot)
                    p.waits("vector", [tl, war_xf if kg == 0 else None])
                    tc_ = p.op("vector", lambda e, b=xfb, kg=kg: e.tensor_copy(
                        xbf[:, kg * 8 * CH:(kg + 1) * 8 * CH],
                        b[:, kg * 8 * CH:(kg + 1) * 8 * CH]), inc="VE")
                    t_cast.append(tc_)
                p.waits("tensor", [t_ps_read])
                t_sx = None
                for kt in range(32):
                    if kt % 8 == 0:
                        p.wait("tensor", t_cast[kt // 8])
                    sl = xbf[:, kt * CH:(kt + 1) * CH]
                    t_sx = p.op("tensor", lambda e, sl=sl, kt=kt: e.matmul(
                        p_s1[:1, :], ones_bf[:, :1], sl, start=(kt == 0), stop=(kt == 31)),
                        inc="TE" if kt == 31 else None)
                t_sq = None
                for kt in range(32):
                    sl = xbf[:, kt * CH:(kt + 1) * CH]
                    sq, war = xsq.next()
                    p.waits("vector", [war, t_cast[kt // 8]])
                    tv = p.op("vector", lambda e, sl=sl, sq=sq: e.tensor_mul(sq[:, :], sl, sl), inc="VE")
                    p.wait("tensor", tv)
                    t_sq = p.op("tensor", lambda e, kt=kt, sq=sq: e.matmul(
                        p_s2[:1, :], ones_bf[:, :1], sq[:, :], start=(kt == 0), stop=(kt == 31)), inc="TE")
                    xsq.done(t_sq)
                mrow, e2r, var, rstd, nmr = (rows[:1, i * CH:(i + 1) * CH] for i in range(5))
                p.waits("vector", [t_sx, t_sq])
                p.op("vector", lambda e, mrow=mrow: e.tensor_scalar_mul(mrow, p_s1[:1, :], 1.0 / D))
                t_ps_read = p.op("vector", lambda e, e2r=e2r: e.tensor_scalar_mul(
                    e2r, p_s2[:1, :], 1.0 / D), inc="VE")
                p.op("vector", lambda e, var=var, mrow=mrow: e.tensor_mul(var, mrow, mrow))
                p.op("vector", lambda e, var=var, e2r=e2r: e.tensor_sub(var, e2r, var))
                t_vr = p.op("vector", lambda e, var=var: e.tensor_scalar_add(var, var, EPS), inc="VE")
                p.wait("scalar", t_vr)
                p.op("scalar", lambda e, rstd=rstd, var=var: e.activation(rstd, var, AF.Ln))
                t_ex = p.op("scalar", lambda e, rstd=rstd: e.activation(
                    rstd, rstd, AF.Exp, scale=-0.5), inc="SE")
                p.wait("vector", t_ex)
                t_nm = p.op("vector", lambda e, nmr=nmr, mrow=mrow, rstd=rstd: e.scalar_tensor_tensor(
                    nmr, mrow, -1.0, rstd, ALU.mult, ALU.mult), inc="VE")
                p.waits("tensor", [t_ex, t_nm, t_pb_read])
                tb1 = p.op("tensor", lambda e, rstd=rstd: e.matmul(
                    p_b1[:, :], onesrow[:1, :], rstd, start=True, stop=True), inc="TE")
                tb2 = p.op("tensor", lambda e, nmr=nmr: e.matmul(
                    p_b2[:, :], onesrow[:1, :], nmr, start=True, stop=True), inc="TE")
                p.waits("vector", [tb1, tb2])
                p.op("vector", lambda e: e.tensor_copy(rbs[:, :], p_b1[:, :]))
                t_pb_read = p.op("vector", lambda e: e.tensor_copy(mbs[:, :], p_b2[:, :]), inc="VE")
                p.wait("vector", t_store)
                tva = None
                for kt in range(32):
                    sl = xfb[:, kt * CH:(kt + 1) * CH]
                    dsl = xlnb[:, kt * CH:(kt + 1) * CH]
                    p.op("vector", lambda e, sl=sl: e.tensor_mul(tmpt[:, :], sl, rbs[:, :]))
                    tva = p.op("vector", lambda e, dsl=dsl: e.tensor_add(dsl, tmpt[:, :], mbs[:, :]),
                               inc="VE")
                xf.done(tva)
                p.wait("scalar", tva)
                apo = bass.AP(s_xln, col0, [[L, 128], [128 * L, 32], [1, CH]])
                t_store = p.dma("scalar", apo, xlnb[:, :], "ST")
            barrier()

        def emit_proj(wT_dram, bias_sb, out_dram, out_cols, m_chunks, tag):
            with ExitStack() as ph:
                wf = Rot([ph.enter_context(nc.sbuf_tensor(f"{tag}wf{i}", [128, 8 * 512], F32)) for i in range(2)])
                wbf = Rot([ph.enter_context(nc.sbuf_tensor(f"{tag}wbf{i}", [128, 32 * 512], BF16)) for i in range(2)])
                xl = Rot([ph.enter_context(nc.sbuf_tensor(f"{tag}xl{i}", [128, 32 * 512], BF16)) for i in range(2)])
                ev = Rot([ph.enter_context(nc.sbuf_tensor(f"{tag}ev{i}", [128, 512], BF16)) for i in range(2)])
                pp = Rot([ph.enter_context(nc.psum_tensor(f"{tag}pp{i}", [128, 512], F32)) for i in range(2)])
                for cb in range(8):
                    wb, war_w = wbf.next()
                    wcast = None
                    for kg in range(4):
                        buf, war = wf.next()
                        p.waits("sync", [war, war_w if kg == 0 else None])
                        ap = bass.AP(wT_dram, (kg * 8 * 128) * D + cb * 512,
                                     [[D, 128], [128 * D, 8], [1, 512]])
                        tl = p.dma("sync", buf[:, :], ap, "B%d" % wf.slot)
                        p.waits("vector", [tl, war_w if kg == 0 else None])
                        wcast = p.op("vector", lambda e, b=buf, wb=wb, kg=kg: e.tensor_copy(
                            wb[:, kg * 8 * 512:(kg + 1) * 8 * 512], b[:, :]), inc="VE")
                        wf.done(wcast)
                    t_mm_last = None
                    for mb in m_chunks:
                        xb, war_x = xl.next()
                        p.wait("sync", war_x)
                        apx = bass.AP(s_xln, mb * 512, [[L, 128], [128 * L, 32], [1, 512]])
                        tx = p.dma("sync", xb[:, :], apx, "C%d" % xl.slot)
                        p.waits("tensor", [tx, wcast])
                        for ct in range(4):
                            ps, war_p = pp.next()
                            p.wait("tensor", war_p)
                            tmm = None
                            for kt in range(32):
                                tmm = p.op("tensor", lambda e, ps=ps, wb=wb, kt=kt, ct=ct, xb=xb: e.matmul(
                                    ps[:, :],
                                    wb[:, kt * 512 + ct * 128:kt * 512 + ct * 128 + 128],
                                    xb[:, kt * 512:(kt + 1) * 512],
                                    start=(kt == 0), stop=(kt == 31)),
                                    inc="TE" if kt == 31 else None)
                            t_mm_last = tmm
                            eb, war_e = ev.next()
                            p.waits("vector", [tmm, war_e])
                            bcol = bias_sb[:, cb * 4 + ct:cb * 4 + ct + 1]
                            tevd = p.op("vector", lambda e, eb=eb, ps=ps, bcol=bcol: e.tensor_scalar_add(
                                eb[:, :], ps[:, :], bcol), inc="VE")
                            pp.done(tevd)
                            p.wait("scalar", tevd)
                            row0 = cb * 512 + ct * 128
                            apo = bass.AP(out_dram, row0 * out_cols + (mb - m_chunks[0]) * 512,
                                          [[out_cols, 128], [1, 512]])
                            tst = p.dma("scalar", apo, eb[:, :], "ST")
                            ev.done(tst)
                        xl.done(t_mm_last)
                    wbf.done(t_mm_last)
            barrier()

        emit_proj(d_wkT, bk_sb, s_kT, L, list(range(8)), "k")
        emit_proj(d_wqT, bq_sb, s_qT, OWN, [0, 1], "q")

        with ExitStack() as ph:
            bv_sb = ph.enter_context(nc.sbuf_tensor("bv_sb", [1, D], BF16))
            t_bv = p.dma("sync", bv_sb[:, :], d_bv[:, :], "LD")
            wf = Rot([ph.enter_context(nc.sbuf_tensor(f"vwf{i}", [128, 8 * 512], F32)) for i in range(2)])
            wbf = Rot([ph.enter_context(nc.sbuf_tensor(f"vwbf{i}", [128, 32 * 512], BF16)) for i in range(2)])
            xl = Rot([ph.enter_context(nc.sbuf_tensor(f"vxl{i}", [128, 32 * 128], BF16)) for i in range(2)])
            ev = Rot([ph.enter_context(nc.sbuf_tensor(f"vev{i}", [128, 512], BF16)) for i in range(2)])
            pp = Rot([ph.enter_context(nc.psum_tensor(f"vpp{i}", [128, 512], F32)) for i in range(2)])
            for cb in range(8):
                wb, war_w = wbf.next()
                wcast = None
                for kg in range(4):
                    buf, war = wf.next()
                    p.waits("sync", [war, war_w if kg == 0 else None])
                    ap = bass.AP(d_wvT, (kg * 8 * 128) * D + cb * 512,
                                 [[D, 128], [128 * D, 8], [1, 512]])
                    tl = p.dma("sync", buf[:, :], ap, "B%d" % wf.slot)
                    p.waits("vector", [tl, war_w if kg == 0 else None])
                    wcast = p.op("vector", lambda e, b=buf, wb=wb, kg=kg: e.tensor_copy(
                        wb[:, kg * 8 * 512:(kg + 1) * 8 * 512], b[:, :]), inc="VE")
                    wf.done(wcast)
                t_last = None
                for mt in range(32):
                    xb, war_x = xl.next()
                    p.wait("sync", war_x)
                    apx = bass.AP(s_xln, mt * 128, [[L, 128], [128 * L, 32], [1, 128]])
                    tx = p.dma("sync", xb[:, :], apx, "C%d" % xl.slot)
                    ps, war_p = pp.next()
                    p.waits("tensor", [tx, wcast, war_p])
                    for kt in range(32):
                        p.op("tensor", lambda e, ps=ps, xb=xb, kt=kt, wb=wb: e.matmul(
                            ps[:, :],
                            xb[:, kt * 128:(kt + 1) * 128],
                            wb[:, kt * 512:(kt + 1) * 512],
                            start=(kt == 0), stop=False))
                    if cb == 0 and mt == 0:
                        p.wait("tensor", t_bv)
                    tmm = p.op("tensor", lambda e, ps=ps, cb=cb: e.matmul(
                        ps[:, :], onesrow_bf[:1, :], bv_sb[:1, cb * 512:(cb + 1) * 512],
                        start=False, stop=True), inc="TE")
                    t_last = tmm
                    xl.done(tmm)
                    eb, war_e = ev.next()
                    p.waits("vector", [tmm, war_e])
                    tevd = p.op("vector", lambda e, eb=eb, ps=ps: e.tensor_copy(eb[:, :], ps[:, :]), inc="VE")
                    pp.done(tevd)
                    p.wait("scalar", tevd)
                    apo = bass.AP(s_v, (mt * 128) * D + cb * 512, [[D, 128], [1, 512]])
                    tst = p.dma("scalar", apo, eb[:, :], "ST")
                    ev.done(tst)
                wbf.done(t_last)
        barrier()

        q_es = ExitStack()
        akron = q_es.enter_context(nc.sbuf_tensor("akron", [128, NH * 2 * 256], BF16))
        qeff = q_es.enter_context(nc.sbuf_tensor("qeff", [128, 32 * OWN], BF16))
        with nc.sbuf_tensor("sb_kbasis", [128, 4 * 2 * 256], F32) as kbasis, \
             nc.sbuf_tensor("bwrow", [1, 48], F32) as bwrow, \
             nc.sbuf_tensor("bwcast", [128, 48], F32) as bwcast, \
             nc.sbuf_tensor("ak_acc", [128, 256], F32) as ak_acc, \
             nc.psum_tensor("p_bw", [128, 48], F32) as p_bw:
            tkb = p.dma("sync", kbasis[:, :], d_kbasis[:, :], "LD")
            tbw = p.dma("sync", bwrow[:, :], d_bw[:, :], "LD")
            p.waits("tensor", [tkb, tbw])
            t_mm = p.op("tensor", lambda e: e.matmul(
                p_bw[:, :], onesrow[:1, :], bwrow[:1, :], start=True, stop=True), inc="TE")
            p.wait("vector", t_mm)
            p.op("vector", lambda e: e.tensor_copy(bwcast[:, :], p_bw[:, :]), inc="VE")
            for h in range(NH):
                for kt2 in range(2):
                    dst = akron[:, (h * 2 + kt2) * 256:(h * 2 + kt2 + 1) * 256]
                    sl = [kbasis[:, (i * 2 + kt2) * 256:(i * 2 + kt2 + 1) * 256] for i in range(4)]
                    sc = [bwcast[:, h * 3 + c:h * 3 + c + 1] for c in range(3)]
                    p.op("vector", lambda e, a=sl[1], s=sc[0], b=sl[0]: e.scalar_tensor_tensor(
                        ak_acc[:, :], a, s, b, ALU.mult, ALU.add))
                    p.op("vector", lambda e, a=sl[2], s=sc[1]: e.scalar_tensor_tensor(
                        ak_acc[:, :], a, s, ak_acc[:, :], ALU.mult, ALU.add))
                    t_ak = p.op("vector", lambda e, a=sl[3], s=sc[2], d=dst: e.scalar_tensor_tensor(
                        d, a, s, ak_acc[:, :], ALU.mult, ALU.add), inc="VE")
            p.wait("tensor", t_ak)
        barrier()
        with ExitStack() as ph:
            qh = Rot([ph.enter_context(nc.sbuf_tensor(f"qh{i}", [128, 2 * OWN], BF16)) for i in range(2)])
            pq = Rot([ph.enter_context(nc.psum_tensor(f"pq{i}", [128, 512], F32)) for i in range(2)])
            for h in range(NH):
                qb, war_q = qh.next()
                p.wait("sync", war_q)
                apq = bass.AP(s_qT, (h * 2 * 128) * OWN, [[OWN, 128], [128 * OWN, 2], [1, OWN]])
                tq = p.dma("sync", qb[:, :], apq, "D%d" % qh.slot)
                t_last = None
                for ct2 in range(2):
                    for lc in range(2):
                        ps, war_p = pq.next()
                        p.waits("tensor", [tq, war_p])
                        tmm = None
                        for kt2 in range(2):
                            tmm = p.op("tensor", lambda e, ps=ps, h=h, kt2=kt2, ct2=ct2, lc=lc, qb=qb: e.matmul(
                                ps[:, :],
                                akron[:, (h * 2 + kt2) * 256 + ct2 * 128:(h * 2 + kt2) * 256 + ct2 * 128 + 128],
                                qb[:, kt2 * OWN + lc * 512:kt2 * OWN + lc * 512 + 512],
                                start=(kt2 == 0), stop=(kt2 == 1)), inc="TE")
                        p.wait("vector", tmm)
                        dst = qeff[:, (h * 2 + ct2) * OWN + lc * 512:(h * 2 + ct2) * OWN + lc * 512 + 512]
                        tev = p.op("vector", lambda e, dst=dst, ps=ps: e.tensor_copy(dst, ps[:, :]), inc="VE")
                        pq.done(tev)
                        t_last = tev
                qh.done(t_last)
        barrier()

        with ExitStack() as ph:
            kh = Rot([ph.enter_context(nc.sbuf_tensor(f"kh{i}", [128, 2 * L], BF16)) for i in range(2)])
            vh = Rot([ph.enter_context(nc.sbuf_tensor(f"vh{i}", [128, 32 * 256], BF16)) for i in range(2)])
            Eb = ph.enter_context(nc.sbuf_tensor("Eb", [128, 32 * 512], BF16))
            Sacc = ph.enter_context(nc.sbuf_tensor("Sacc", [128, 512], F32))
            ksq = Rot([ph.enter_context(nc.sbuf_tensor(f"ksq{i}", [128, 512], BF16)) for i in range(2)])
            smax = ph.enter_context(nc.sbuf_tensor("smax", [1, 8], F32))
            rrow = ph.enter_context(nc.sbuf_tensor("rrow", [1, 512], F32))
            negb = ph.enter_context(nc.sbuf_tensor("negb", [128, 1], F32))
            rb_sb = ph.enter_context(nc.sbuf_tensor("rb_sb", [128, 512], F32))
            onorm = Rot([ph.enter_context(nc.sbuf_tensor(f"onorm{i}", [128, 512], F32)) for i in range(2)])
            p_log = Rot([ph.enter_context(nc.psum_tensor(f"p_log{i}", [128, 512], F32)) for i in range(2)])
            p_out = [ph.enter_context(nc.psum_tensor(f"p_out{i}", [128, 512], F32)) for i in range(2)]
            p_row = ph.enter_context(nc.psum_tensor("p_row", [1, 512], F32))
            p_rb = ph.enter_context(nc.psum_tensor("p_rb", [128, 512], F32))

            t_k = {}
            t_kv = {}
            t_kread_done = {}

            def load_kv(h):
                if h >= NH:
                    return
                kb, war1 = kh.next()
                vb, war2 = vh.next()
                p.waits("sync", [war1, war2, t_kread_done.get(h - 2)])
                apk = bass.AP(s_kT, (h * 2 * 128) * L, [[L, 128], [128 * L, 2], [1, L]])
                t_k[h] = p.dma("sync", kb[:, :], apk, "K%d" % kh.slot)
                apv = bass.AP(s_v, h * 256, [[D, 128], [128 * D, 32], [1, 256]])
                t_kv[h] = p.dma("sync", vb[:, :], apv, "V%d" % vh.slot)
                return kb, vb

            cur = load_kv(0)
            t_p_row_free = None
            t_p_rb_free = None
            t_out_free = [None, None]
            t_lnk = None
            for h in range(NH):
                kb, vb = cur
                cur = load_kv(h + 1)
                p.wait("vector", t_lnk)
                t_r = None
                for mc8 in range(8):
                    tq = None
                    for kt2 in range(2):
                        sl = kb[:, kt2 * L + mc8 * 512:kt2 * L + mc8 * 512 + 512]
                        sq, war = ksq.next()
                        p.waits("vector", [t_k[h], war])
                        tv = p.op("vector", lambda e, sl=sl, sq=sq: e.tensor_mul(sq[:, :], sl, sl), inc="VE")
                        p.waits("tensor", [tv, t_p_row_free if mc8 == 0 and kt2 == 0 else None])
                        tq = p.op("tensor", lambda e, kt2=kt2, sq=sq: e.matmul(
                            p_row[:1, :], ones_bf[:, :1], sq[:, :],
                            start=(kt2 == 0), stop=(kt2 == 1)), inc="TE")
                        ksq.done(tq)
                    p.wait("vector", tq)
                    dst = smax[:1, 0:1] if mc8 == 0 else smax[:1, 1:2]
                    t_r = p.op("vector", lambda e, dst=dst: e.reduce_max(dst, p_row[:1, :], axis=AX.X), inc="VE")
                    if mc8 > 0:
                        t_r = p.op("vector", lambda e: e.tensor_max(
                            smax[:1, 0:1], smax[:1, 0:1], smax[:1, 1:2]), inc="VE")
                    t_p_row_free = t_r
                p.wait("scalar", t_r)
                t_lnk = p.op("scalar", lambda e: e.activation(smax[:1, 2:3], smax[:1, 0:1], AF.Ln), inc="SE")

                for lc in range(2):
                    tq = None
                    for kt2 in range(2):
                        sl = qeff[:, (h * 2 + kt2) * OWN + lc * 512:(h * 2 + kt2) * OWN + lc * 512 + 512]
                        sq, war = ksq.next()
                        p.wait("vector", war)
                        tv = p.op("vector", lambda e, sl=sl, sq=sq: e.tensor_mul(sq[:, :], sl, sl), inc="VE")
                        p.waits("tensor", [tv, t_p_row_free if kt2 == 0 else None])
                        tq = p.op("tensor", lambda e, kt2=kt2, sq=sq: e.matmul(
                            p_row[:1, :], ones_bf[:, :1], sq[:, :],
                            start=(kt2 == 0), stop=(kt2 == 1)), inc="TE")
                        ksq.done(tq)
                    p.wait("vector", tq)
                    t_qm = p.op("vector", lambda e: e.reduce_max(smax[:1, 3:4], p_row[:1, :], axis=AX.X), inc="VE")
                    t_p_row_free = t_qm
                    p.waits("scalar", [t_qm, t_lnk])
                    t_l2 = p.op("scalar", lambda e: e.activation(smax[:1, 3:4], smax[:1, 3:4], AF.Ln), inc="SE")
                    p.wait("vector", t_l2)
                    t_ad = p.op("vector", lambda e: e.tensor_add(smax[:1, 3:4], smax[:1, 3:4], smax[:1, 2:3]), inc="VE")
                    p.wait("scalar", t_ad)
                    t_eb = p.op("scalar", lambda e: e.activation(
                        smax[:1, 3:4], smax[:1, 3:4], AF.Exp, scale=0.5), inc="SE")
                    p.wait("vector", t_eb)
                    t_nb = p.op("vector", lambda e: e.tensor_scalar_mul(smax[:1, 3:4], smax[:1, 3:4], -1.02), inc="VE")
                    p.waits("tensor", [t_nb, t_p_rb_free])
                    t_bc = p.op("tensor", lambda e: e.matmul(
                        p_rb[:, 0:1], onesrow[:1, :], smax[:1, 3:4], start=True, stop=True), inc="TE")
                    p.wait("vector", t_bc)
                    t_nbc = p.op("vector", lambda e: e.tensor_copy(negb[:, :], p_rb[:, 0:1]), inc="VE")

                    t_exp = {}
                    t_sacc = None
                    t_av = [None, None]
                    for mt in range(32):
                        pl, war_l = p_log.next()
                        p.waits("tensor", [war_l] + ([t_out_free[0], t_out_free[1]] if mt == 0 else []))
                        tl = None
                        for kt2 in range(2):
                            tl = p.op("tensor", lambda e, pl=pl, kb=kb, kt2=kt2, mt=mt, h=h, lc=lc: e.matmul(
                                pl[:, :],
                                kb[:, kt2 * L + mt * 128:kt2 * L + mt * 128 + 128],
                                qeff[:, (h * 2 + kt2) * OWN + lc * 512:(h * 2 + kt2) * OWN + lc * 512 + 512],
                                start=(kt2 == 0), stop=(kt2 == 1)), inc="TE")
                        p.waits("scalar", [tl, t_nbc if mt == 0 else None])
                        t_exp[mt] = p.op("scalar", lambda e, pl=pl, mt=mt: e.activation(
                            Eb[:, mt * 512:(mt + 1) * 512], pl[:, :], AF.Exp, bias=negb[:, 0:1]), inc="SE")
                        p_log.done(t_exp[mt])
                        p.waits("vector", [t_exp[mt]] + ([t_p_row_free] if mt == 0 else []))
                        if mt == 0:
                            t_sacc = p.op("vector", lambda e: e.tensor_copy(Sacc[:, :], Eb[:, 0:512]), inc="VE")
                        else:
                            t_sacc = p.op("vector", lambda e, mt=mt: e.tensor_add(
                                Sacc[:, :], Sacc[:, :], Eb[:, mt * 512:(mt + 1) * 512]), inc="VE")
                        p.waits("tensor", [t_exp[mt], t_kv[h] if mt == 0 else None])
                        for ct in range(2):
                            t_av[ct] = p.op("tensor", lambda e, po=p_out[ct], vb=vb, mt=mt, ct=ct: e.matmul(
                                po[:, :],
                                vb[:, mt * 256 + ct * 128:mt * 256 + ct * 128 + 128],
                                Eb[:, mt * 512:(mt + 1) * 512],
                                start=(mt == 0), stop=(mt == 31)), inc="TE" if mt == 31 else None)
                    t_kread_done[h] = t_av[1]
                    p.waits("tensor", [t_sacc, t_p_row_free])
                    t_sr = p.op("tensor", lambda e: e.matmul(
                        p_row[:1, :], ones_f[:, :1], Sacc[:, :], start=True, stop=True), inc="TE")
                    p.wait("vector", t_sr)
                    t_rc = p.op("vector", lambda e: e.reciprocal(rrow[:1, :], p_row[:1, :]), inc="VE")
                    t_p_row_free = t_rc
                    p.wait("tensor", t_rc)
                    t_rb = p.op("tensor", lambda e: e.matmul(
                        p_rb[:, :], onesrow[:1, :], rrow[:1, :], start=True, stop=True), inc="TE")
                    p.wait("vector", t_rb)
                    t_rbc = p.op("vector", lambda e: e.tensor_copy(rb_sb[:, :], p_rb[:, :]), inc="VE")
                    t_p_rb_free = t_rbc
                    for ct in range(2):
                        ob, war_o = onorm.next()
                        p.waits("vector", [t_av[ct], t_rbc, war_o])
                        tno = p.op("vector", lambda e, ob=ob, po=p_out[ct]: e.tensor_mul(
                            ob[:, :], po[:, :], rb_sb[:, :]), inc="VE")
                        t_out_free[ct] = tno
                        p.wait("scalar", tno)
                        row0 = h * 256 + ct * 128
                        apo = bass.AP(s_attn, row0 * OWN + lc * 512, [[OWN, 128], [1, 512]])
                        tst = p.dma("scalar", apo, ob[:, :], "ST")
                        onorm.done(tst)
        q_es.close()
        barrier()

        CH4 = 256
        with ExitStack() as ph:
            ain = Rot([ph.enter_context(nc.sbuf_tensor(f"ain{i}", [128, 8 * CH4], F32)) for i in range(2)])
            bin_ = Rot([ph.enter_context(nc.sbuf_tensor(f"bin{i}", [128, 8 * CH4], F32)) for i in range(2)])
            x2f = ph.enter_context(nc.sbuf_tensor("x2f", [128, 32 * CH4], F32))
            x2b = ph.enter_context(nc.sbuf_tensor("x2b", [128, 32 * CH4], BF16))
            xsq = Rot([ph.enter_context(nc.sbuf_tensor(f"xsq4{i}", [128, CH4], BF16)) for i in range(2)])
            rows = ph.enter_context(nc.sbuf_tensor("rows4", [1, 6 * CH4], F32))
            rbs = ph.enter_context(nc.sbuf_tensor("rbs4", [128, CH4], F32))
            mbs = ph.enter_context(nc.sbuf_tensor("mbs4", [128, CH4], F32))
            tmpt = ph.enter_context(nc.sbuf_tensor("tmpt4", [128, CH4], F32))
            xlnb = ph.enter_context(nc.sbuf_tensor("xlnb4", [128, 32 * CH4], BF16))
            p_s1 = ph.enter_context(nc.psum_tensor("p4_s1", [1, CH4], F32))
            p_s2 = ph.enter_context(nc.psum_tensor("p4_s2", [1, CH4], F32))
            p_b1 = ph.enter_context(nc.psum_tensor("p4_b1", [128, CH4], F32))
            p_b2 = ph.enter_context(nc.psum_tensor("p4_b2", [128, CH4], F32))
            t_store = [None, None]
            t_pb_read = None
            t_ps_read = None
            for mc in range(4):
                t_add = []
                for kg in range(4):
                    ab, wa = ain.next()
                    bb, wb_ = bin_.next()
                    p.waits("sync", [wa, wb_])
                    apa = bass.AP(d_xT, (kg * 8 * 128) * L + mc * CH4,
                                  [[L, 128], [128 * L, 8], [1, CH4]])
                    ta = p.dma("sync", ab[:, :], apa, "E%d" % ain.slot)
                    apb = bass.AP(s_attn, (kg * 8 * 128) * OWN + mc * CH4,
                                  [[OWN, 128], [128 * OWN, 8], [1, CH4]])
                    tb = p.dma("sync", bb[:, :], apb, "F%d" % bin_.slot)
                    p.waits("vector", [ta, tb, t_store[0] if kg == 0 else None])
                    tx2 = p.op("vector", lambda e, ab=ab, bb=bb, kg=kg: e.tensor_add(
                        x2f[:, kg * 8 * CH4:(kg + 1) * 8 * CH4], ab[:, :], bb[:, :]), inc="VE")
                    ain.done(tx2)
                    bin_.done(tx2)
                    tcb = p.op("vector", lambda e, kg=kg: e.tensor_copy(
                        x2b[:, kg * 8 * CH4:(kg + 1) * 8 * CH4],
                        x2f[:, kg * 8 * CH4:(kg + 1) * 8 * CH4]), inc="VE")
                    t_add.append(tcb)
                p.wait("scalar", t_add[3])
                apx2 = bass.AP(s_x2, mc * CH4, [[OWN, 128], [128 * OWN, 32], [1, CH4]])
                t_store[0] = p.dma("scalar", apx2, x2f[:, :], "ST")
                p.waits("tensor", [t_ps_read])
                t_sx = None
                for kt in range(32):
                    if kt % 8 == 0:
                        p.wait("tensor", t_add[kt // 8])
                    sl = x2b[:, kt * CH4:(kt + 1) * CH4]
                    t_sx = p.op("tensor", lambda e, sl=sl, kt=kt: e.matmul(
                        p_s1[:1, :], ones_bf[:, :1], sl, start=(kt == 0), stop=(kt == 31)),
                        inc="TE" if kt == 31 else None)
                t_sq = None
                for kt in range(32):
                    sl = x2b[:, kt * CH4:(kt + 1) * CH4]
                    sq, war = xsq.next()
                    p.wait("vector", war)
                    tv = p.op("vector", lambda e, sl=sl, sq=sq: e.tensor_mul(sq[:, :], sl, sl), inc="VE")
                    p.wait("tensor", tv)
                    t_sq = p.op("tensor", lambda e, kt=kt, sq=sq: e.matmul(
                        p_s2[:1, :], ones_bf[:, :1], sq[:, :], start=(kt == 0), stop=(kt == 31)), inc="TE")
                    xsq.done(t_sq)
                mrow, e2r, var, rstd, nmr = (rows[:1, i * CH4:(i + 1) * CH4] for i in range(5))
                p.waits("vector", [t_sx, t_sq])
                p.op("vector", lambda e, mrow=mrow: e.tensor_scalar_mul(mrow, p_s1[:1, :], 1.0 / D))
                t_ps_read = p.op("vector", lambda e, e2r=e2r: e.tensor_scalar_mul(
                    e2r, p_s2[:1, :], 1.0 / D), inc="VE")
                p.op("vector", lambda e, var=var, mrow=mrow: e.tensor_mul(var, mrow, mrow))
                p.op("vector", lambda e, var=var, e2r=e2r: e.tensor_sub(var, e2r, var))
                t_vr = p.op("vector", lambda e, var=var: e.tensor_scalar_add(var, var, EPS), inc="VE")
                p.wait("scalar", t_vr)
                p.op("scalar", lambda e, rstd=rstd, var=var: e.activation(rstd, var, AF.Ln))
                t_ex = p.op("scalar", lambda e, rstd=rstd: e.activation(
                    rstd, rstd, AF.Exp, scale=-0.5), inc="SE")
                p.wait("vector", t_ex)
                t_nm = p.op("vector", lambda e, nmr=nmr, mrow=mrow, rstd=rstd: e.scalar_tensor_tensor(
                    nmr, mrow, -1.0, rstd, ALU.mult, ALU.mult), inc="VE")
                p.waits("tensor", [t_ex, t_nm, t_pb_read])
                tb1 = p.op("tensor", lambda e, rstd=rstd: e.matmul(
                    p_b1[:, :], onesrow[:1, :], rstd, start=True, stop=True), inc="TE")
                tb2 = p.op("tensor", lambda e, nmr=nmr: e.matmul(
                    p_b2[:, :], onesrow[:1, :], nmr, start=True, stop=True), inc="TE")
                p.waits("vector", [tb1, tb2])
                p.op("vector", lambda e: e.tensor_copy(rbs[:, :], p_b1[:, :]))
                t_pb_read = p.op("vector", lambda e: e.tensor_copy(mbs[:, :], p_b2[:, :]), inc="VE")
                p.wait("vector", t_store[1])
                tva = None
                for kt in range(32):
                    sl = x2f[:, kt * CH4:(kt + 1) * CH4]
                    dsl = xlnb[:, kt * CH4:(kt + 1) * CH4]
                    p.op("vector", lambda e, sl=sl: e.tensor_mul(tmpt[:, :], sl, rbs[:, :]))
                    tva = p.op("vector", lambda e, dsl=dsl: e.tensor_add(dsl, tmpt[:, :], mbs[:, :]), inc="VE")
                p.wait("scalar", tva)
                apo = bass.AP(s_xln2, mc * CH4, [[OWN, 128], [128 * OWN, 32], [1, CH4]])
                t_store[1] = p.dma("scalar", apo, xlnb[:, :], "ST")
            barrier()

        with ExitStack() as ph:
            xlp = ph.enter_context(nc.sbuf_tensor("xlp", [128, 32 * OWN], BF16))
            gones = ph.enter_context(nc.sbuf_tensor("sb_gones", [128, 32 * 32], BF16))
            giones = ph.enter_context(nc.sbuf_tensor("sb_giones", [128, 11 * 86], BF16))
            xg_sb = ph.enter_context(nc.sbuf_tensor("xg_sb", [32, OWN], F32))

            with ExitStack() as bh:
                qzb = Rot([bh.enter_context(nc.sbuf_tensor(f"qzb{i}", [32, 128], I32)) for i in range(2)])
                scb = Rot([bh.enter_context(nc.sbuf_tensor(f"scb{i}", [32, 1024], F32)) for i in range(2)])
                zpb = bh.enter_context(nc.sbuf_tensor("zpb", [32, 1024], I32))
                snb = bh.enter_context(nc.sbuf_tensor("snb", [32, 1024], F32))
                m2b = Rot([bh.enter_context(nc.sbuf_tensor(f"m2b{i}", [32, 1024], F32)) for i in range(2)])
                for which, qz_d, scp_d, m2_d in (("g", d_qzg, d_scg_p, s_m2g),
                                                 ("u", d_qzu, d_scu_p, s_m2u)):
                    for kb2 in range(11):
                        Mj = 128 if kb2 < 10 else 96
                        qb, warz = qzb.next()
                        p.wait("sync", warz)
                        tz = p.dma("sync", qb[:, 0:Mj],
                                   bass.AP(qz_d, kb2 * 128, [[1376, 32], [1, Mj]]), "J%d" % qzb.slot)
                        sb, wars = scb.next()
                        p.wait("sync", wars)
                        ts = p.dma("sync", sb[:, :],
                                   bass.AP(scp_d, kb2 * 1024, [[NT_I * 128, 32], [1, 1024]]), "M%d" % scb.slot)
                        tun = None
                        p.waits("vector", [tz, ts])
                        for t8 in range(8):
                            tun = p.op("vector", lambda e, qb=qb, t8=t8, Mj=Mj: e.tensor_scalar(
                                zpb[:, t8 * 128:t8 * 128 + Mj], qb[:, 0:Mj], 4 * t8, 15,
                                ALU.logical_shift_right, ALU.bitwise_and), inc="VE")
                        qzb.done(tun)
                        p.op("vector", lambda e, sb=sb: e.tensor_scalar_mul(snb[:, :], sb[:, :], -1.0))
                        mb, warm = m2b.next()
                        p.wait("vector", warm)
                        tm = p.op("vector", lambda e, mb=mb: e.scalar_tensor_tensor(
                            mb[:, :], zpb[:, :], 1.0, snb[:, :], ALU.add, ALU.mult), inc="VE")
                        scb.done(tm)
                        p.wait("scalar", tm)
                        tst = p.dma("scalar", bass.AP(m2_d, kb2 * 1024, [[NT_I * 128, 32], [1, 1024]]),
                                    mb[:, :], "ST")
                        m2b.done(tst)
                t_m2done = p.tok("ST")
            p.wait("sync", p.tok("VE"))

            p.dma("sync", gones[:, :], d_gones[:, :], "LD")
            tgi = p.dma("sync", giones[:, :], d_giones[:, :], "LD")
            tgo = tgi
            t_xlp = None
            for kbt in range(4):
                for s in range(8):
                    ti = kbt * 8 + s
                    apx = bass.AP(s_xln2, (1024 * kbt + s) * OWN, [[8 * OWN, 128], [1, OWN]])
                    t_xlp = p.dma("sync", xlp[:, ti * OWN:(ti + 1) * OWN], apx, "XL")

            qwb = Rot([ph.enter_context(nc.sbuf_tensor(f"qwb{i}", [128, 1024], I32)) for i in range(2)])
            w4 = ph.enter_context(nc.sbuf_tensor("w4", [128, 1024], I32))
            wbf = ph.enter_context(nc.sbuf_tensor("wbfg", [128, 32 * 1024], BF16))
            gsil = ph.enter_context(nc.sbuf_tensor("gsil", [128, 8 * OWN], F32))
            hbuf = Rot([ph.enter_context(nc.sbuf_tensor(f"hbuf{i}", [128, 512], BF16)) for i in range(2)])
            sxsb = ph.enter_context(nc.sbuf_tensor("sxsb", [128, 1024], F32))
            m2sl = Rot([ph.enter_context(nc.sbuf_tensor(f"m2sl{i}", [32, 128], F32)) for i in range(2)])
            sct_b = Rot([ph.enter_context(nc.sbuf_tensor(f"sctb{i}", [8, 1024], F32)) for i in range(2)])
            p_sex = Rot([ph.enter_context(nc.psum_tensor(f"p_sex{i}", [128, 1024], F32)) for i in range(2)])
            p_h = Rot([ph.enter_context(nc.psum_tensor(f"p_h{i}", [128, 512], F32)) for i in range(2)])
            p_hg = ph.enter_context(nc.psum_tensor("p_hg", [86, OWN], F32))

            p.waits("tensor", [t_xlp, tgo])
            t_xg_mm = [None, None]
            for tc in range(2):
                psl = p_h.bufs[tc]
                for kt in range(32):
                    t_xg_mm[tc] = p.op("tensor", lambda e, psl=psl, kt=kt, tc=tc: e.matmul(
                        psl[0:32, :], gones[:, kt * 32:(kt + 1) * 32],
                        xlp[:, kt * OWN + tc * 512:kt * OWN + tc * 512 + 512],
                        start=(kt == 0), stop=(kt == 31)), inc="TE" if kt == 31 else None)
            p.waits("vector", t_xg_mm)
            t_xge = None
            for tc in range(2):
                t_xge = p.op("vector", lambda e, tc=tc, psl=p_h.bufs[tc]: e.tensor_copy(
                    xg_sb[:, tc * 512:(tc + 1) * 512], psl[0:32, :]), inc="VE")
            p_h.free_tok[0] = t_xge
            p_h.free_tok[1] = t_xge

            t_gsil_ev = {}
            for kb2 in range(11):
                W = 1024 if kb2 < 10 else 768
                Mj = 128 if kb2 < 10 else 96
                for which in ("g", "u"):
                    qw_d = d_qwg if which == "g" else d_qwu
                    sct_d = d_scg_t if which == "g" else d_scu_t
                    m2_d = s_m2g if which == "g" else s_m2u
                    t_wready = None
                    for kbt in range(4):
                        qb, warq = qwb.next()
                        p.wait("sync", warq)
                        apq = bass.AP(qw_d, (kbt * 128) * INTER + kb2 * 1024, [[INTER, 128], [1, W]])
                        tq = p.dma("sync", qb[:, 0:W], apq, "G%d" % qwb.slot)
                        sb, warsc = sct_b.next()
                        p.wait("sync", warsc)
                        tsc = p.dma("sync", sb[:8, 0:W],
                                    bass.AP(sct_d, kbt * INTER + kb2 * 1024, [[4 * INTER, 8], [1, W]]), "H%d" % sct_b.slot)
                        ps, warp = p_sex.next()
                        p.waits("tensor", [tsc, warp])
                        tse = None
                        for half in range((W + 511) // 512):
                            w0 = half * 512
                            w1 = min(W, w0 + 512)
                            tse = p.op("tensor", lambda e, ps=ps, sb=sb, w0=w0, w1=w1: e.matmul(
                                ps[:, w0:w1], exp8[:8, :], sb[:8, w0:w1], start=True, stop=True), inc="TE")
                        sct_b.done(tse)
                        p.waits("vector", [tse, tq])
                        tcp = p.op("vector", lambda e, ps=ps, W=W: e.tensor_copy(
                            sxsb[:, 0:W], ps[:, 0:W]), inc="VE")
                        p_sex.done(tcp)
                        for s in range(8):
                            p.op("vector", lambda e, qb=qb, s=s, W=W: e.tensor_scalar(
                                w4[:, 0:W], qb[:, 0:W], 4 * s, 15,
                                ALU.logical_shift_right, ALU.bitwise_and))
                            dst = wbf[:, (kbt * 8 + s) * 1024:(kbt * 8 + s) * 1024 + W]
                            t_wready = p.op("vector", lambda e, dst=dst, W=W: e.tensor_mul(
                                dst, w4[:, 0:W], sxsb[:, 0:W]), inc="VE")
                        qwb.done(t_wready)
                    p.waits("tensor", [t_wready, t_xge])
                    for s2 in range(8):
                        msl, warm = m2sl.next()
                        p.waits("sync", [warm, t_m2done if kb2 == 0 and s2 == 0 else None])
                        tile_i = kb2 * 8 + s2
                        tml = p.dma("sync", msl[:, 0:Mj],
                                    bass.AP(m2_d, tile_i * 128, [[NT_I * 128, 32], [1, Mj]]), "I%d" % m2sl.slot)
                        for tc in range(2):
                            psh, warh = p_h.next()
                            p.wait("tensor", warh)
                            for ktile in range(32):
                                kbt, s = divmod(ktile, 8)
                                lh = wbf[:, (kbt * 8 + s) * 1024 + s2:(kbt * 8 + s) * 1024 + W:8]
                                p.op("tensor", lambda e, psh=psh, lh=lh, ktile=ktile, Mj=Mj, kbt=kbt, s=s, tc=tc: e.matmul(
                                    psh[0:Mj, :], lh,
                                    xlp[:, (kbt * 8 + s) * OWN + tc * 512:(kbt * 8 + s) * OWN + tc * 512 + 512],
                                    start=(ktile == 0), stop=False))
                            p.wait("tensor", tml)
                            tmm = p.op("tensor", lambda e, psh=psh, msl=msl, Mj=Mj, tc=tc: e.matmul(
                                psh[0:Mj, :], msl[:, 0:Mj], xg_sb[:, tc * 512:(tc + 1) * 512],
                                start=False, stop=True), inc="TE")
                            m2sl.done(tmm)
                            if which == "g":
                                p.wait("scalar", tmm)
                                dst = gsil[:, s2 * OWN + tc * 512:s2 * OWN + tc * 512 + 512]
                                tev = p.op("scalar", lambda e, dst=dst, psh=psh, Mj=Mj: e.activation(
                                    dst[0:Mj, :], psh[0:Mj, :], AF.Silu), inc="SE")
                                t_gsil_ev[(s2, tc)] = tev
                                p_h.done(tev)
                            else:
                                hb, warhb = hbuf.next()
                                p.waits("vector", [tmm, t_gsil_ev[(s2, tc)], warhb])
                                th = p.op("vector", lambda e, hb=hb, psh=psh, Mj=Mj, s2=s2, tc=tc: e.tensor_mul(
                                    hb[0:Mj, :], psh[0:Mj, :],
                                    gsil[0:Mj, s2 * OWN + tc * 512:s2 * OWN + tc * 512 + 512]), inc="VE")
                                p_h.done(th)
                                p.waits("tensor", [th, tgi if tile_i == 0 else None])
                                glh = giones[:, kb2 * 86:(kb2 + 1) * 86]
                                is_first = (kb2 == 0 and s2 == 0)
                                is_last = (kb2 == 10 and s2 == 7)
                                thg = p.op("tensor", lambda e, glh=glh, hb=hb, Mj=Mj, tc=tc,
                                           st=is_first, sp=is_last: e.matmul(
                                    p_hg[:, tc * 512:(tc + 1) * 512], glh[0:Mj, :], hb[0:Mj, :],
                                    start=st, stop=sp), inc="TE")
                                p.wait("scalar", thg)
                                aph = bass.AP(s_hT, (tile_i * 128) * OWN + tc * 512, [[OWN, Mj], [1, 512]])
                                tst = p.dma("scalar", aph, hb[0:Mj, :], "ST")
                                hbuf.done([tst, thg])
                                if is_last and tc == 1:
                                    t_hg_last = thg
            p.wait("vector", t_hg_last)
            t_hge = p.op("vector", lambda e: e.tensor_copy(hg_sb[:, :], p_hg[:, :]), inc="VE")
            barrier()

        with ExitStack() as ph:
            m2nd = ph.enter_context(nc.sbuf_tensor("m2nd", [86, D], BF16))
            with ExitStack() as bh:
                scd_n = bh.enter_context(nc.sbuf_tensor("scd_n", [86, D], F32))
                qzd = bh.enter_context(nc.sbuf_tensor("sb_qzd", [86, 512], I32))
                zt6 = bh.enter_context(nc.sbuf_tensor("zt6", [86, D], I32))
                sn6 = bh.enter_context(nc.sbuf_tensor("sn6", [86, D], F32))
                tzd = p.dma("sync", qzd[:, :], d_qzd[:, :], "LD")
                tsd = p.dma("sync", scd_n[:, :], d_scd[:, :], "LD")
                p.waits("vector", [tsd])
                for t8 in range(8):
                    p.op("vector", lambda e, t8=t8: e.tensor_scalar(
                        zt6[:, t8:D:8], qzd[:, 0:512], 4 * t8, 15,
                        ALU.logical_shift_right, ALU.bitwise_and))
                p.op("vector", lambda e: e.tensor_scalar_mul(sn6[:, :], scd_n[:, :], -1.0))
                t_m2d = p.op("vector", lambda e: e.scalar_tensor_tensor(
                    m2nd[:, :], zt6[:, :], 1.0, sn6[:, :], ALU.add, ALU.mult), inc="VE")
            p.wait("sync", t_m2d)

            hT = ph.enter_context(nc.sbuf_tensor("hT", [128, NT_I * OWN], BF16))
            qwb6 = Rot([ph.enter_context(nc.sbuf_tensor(f"qwb6{i}", [128, 384], I32)) for i in range(2)])
            w46 = ph.enter_context(nc.sbuf_tensor("w46", [128, 384], I32))
            wdb = Rot([ph.enter_context(nc.sbuf_tensor(f"wdb{i}", [128, 384], BF16)) for i in range(2)])
            sx6 = ph.enter_context(nc.sbuf_tensor("sx6", [128, 384], F32))
            sct6 = Rot([ph.enter_context(nc.sbuf_tensor(f"sct6{i}", [8, 384], F32)) for i in range(2)])
            x2t = Rot([ph.enter_context(nc.sbuf_tensor(f"x2t{i}", [128, 512], F32)) for i in range(2)])
            ot = Rot([ph.enter_context(nc.sbuf_tensor(f"ot{i}", [128, 512], F32)) for i in range(2)])
            p_o = [ph.enter_context(nc.psum_tensor(f"p_o{i}", [128, 512], F32)) for i in range(6)]
            p_sx6 = Rot([ph.enter_context(nc.psum_tensor(f"p_sx6{i}", [128, 384], F32)) for i in range(2)])

            t_hl = None
            for ti in range(NT_I):
                rows_n = 128 if ti < 80 else 96
                aph = bass.AP(s_hT, (ti * 128) * OWN, [[OWN, rows_n], [1, OWN]])
                t_hl = p.dma("sync", hT[0:rows_n, ti * OWN:(ti + 1) * OWN], aph, "HL")

            db_list = [(0, 3), (3, 3), (6, 3), (9, 3), (12, 3), (15, 3),
                       (18, 3), (21, 3), (24, 3), (27, 3), (30, 2)]
            t_ofree = {}
            for dbi, (dm0, ndm) in enumerate(db_list):
                col0 = dm0 * 128
                ncol = ndm * 128
                ic = 0
                t_dmm = None
                for kb2 in range(11):
                    Mj = 128 if kb2 < 10 else 96
                    qb, warq = qwb6.next()
                    p.wait("sync", warq)
                    apq = bass.AP(d_qwd, (kb2 * 128) * D + col0, [[D, Mj], [1, ncol]])
                    tq = p.dma("sync", qb[0:Mj, 0:ncol], apq, "G%d" % qwb6.slot)
                    sb, warsc = sct6.next()
                    p.wait("sync", warsc)
                    tsc = p.dma("sync", sb[:8, 0:ncol],
                                bass.AP(d_scd_t, kb2 * D + col0, [[11 * D, 8], [1, ncol]]), "H%d" % sct6.slot)
                    ps, warp = p_sx6.next()
                    p.waits("tensor", [tsc, warp, t_hl if dbi == 0 and kb2 == 0 else None])
                    tse = p.op("tensor", lambda e, ps=ps, sb=sb, ncol=ncol: e.matmul(
                        ps[:, 0:ncol], exp8[:8, :], sb[:8, 0:ncol], start=True, stop=True), inc="TE")
                    sct6.done(tse)
                    p.waits("vector", [tse, tq])
                    tcp = p.op("vector", lambda e, ps=ps, ncol=ncol: e.tensor_copy(
                        sx6[:, 0:ncol], ps[:, 0:ncol]), inc="VE")
                    p_sx6.done(tcp)
                    tw = None
                    for s2 in range(8):
                        wd, warw = wdb.next()
                        p.wait("vector", warw)
                        p.op("vector", lambda e, qb=qb, s2=s2, Mj=Mj, ncol=ncol: e.tensor_scalar(
                            w46[0:Mj, 0:ncol], qb[0:Mj, 0:ncol], 4 * s2, 15,
                            ALU.logical_shift_right, ALU.bitwise_and))
                        tw = p.op("vector", lambda e, wd=wd, Mj=Mj, ncol=ncol: e.tensor_mul(
                            wd[0:Mj, 0:ncol], w46[0:Mj, 0:ncol], sx6[0:Mj, 0:ncol]), inc="VE")
                        tile_i = kb2 * 8 + s2
                        p.waits("tensor", [tw] + ([t_ofree.get((dbi - 1, dmi2, tc2))
                                                   for dmi2 in range(3) for tc2 in range(2)]
                                                  if ic == 0 else []))
                        for dmi in range(ndm):
                            for tc in range(2):
                                t_dmm = p.op("tensor", lambda e, po=p_o[dmi * 2 + tc], wd=wd, dmi=dmi,
                                             tc=tc, Mj=Mj, tile_i=tile_i, ic=ic: e.matmul(
                                    po[:, :], wd[0:Mj, dmi * 128:(dmi + 1) * 128],
                                    hT[0:Mj, tile_i * OWN + tc * 512:tile_i * OWN + tc * 512 + 512],
                                    start=(ic == 0), stop=False),
                                    inc="TE" if (dmi == ndm - 1 and tc == 1) else None)
                        wdb.done(t_dmm)
                        ic += 1
                    qwb6.done(tw)
                p.waits("tensor", [t_m2d, t_hge])
                tlast = {}
                for dmi in range(ndm):
                    for tc in range(2):
                        msl = m2nd[:, col0 + dmi * 128:col0 + (dmi + 1) * 128]
                        tlast[(dmi, tc)] = p.op("tensor", lambda e, po=p_o[dmi * 2 + tc], msl=msl, tc=tc: e.matmul(
                            po[:, :], msl, hg_sb[:, tc * 512:(tc + 1) * 512],
                            start=False, stop=True), inc="TE")
                for dmi in range(ndm):
                    for tc in range(2):
                        xb, warx = x2t.next()
                        p.wait("sync", warx)
                        apx = bass.AP(s_x2, (col0 + dmi * 128) * OWN + tc * 512, [[OWN, 128], [1, 512]])
                        tx = p.dma("sync", xb[:, :], apx, "E%d" % x2t.slot)
                        ob, waro = ot.next()
                        p.waits("vector", [tx, tlast[(dmi, tc)], waro])
                        tno = p.op("vector", lambda e, ob=ob, xb=xb, po=p_o[dmi * 2 + tc]: e.tensor_add(
                            ob[:, :], po[:, :], xb[:, :]), inc="VE")
                        x2t.done(tno)
                        t_ofree[(dbi, dmi, tc)] = tno
                        p.wait("scalar", tno)
                        apo = bass.AP(d_out, (col0 + dmi * 128) * OWN + tc * 512, [[OWN, 128], [1, 512]])
                        tst = p.dma("scalar", apo, ob[:, :], "ST")
                        ot.done(tst)
            p.wait("sync", p.tok("ST"))

        with nc.Block() as block:
            @block.sync
            def _(e):
                for f in p.q["sync"]:
                    f(e)

            @block.tensor
            def _(e):
                for f in p.q["tensor"]:
                    f(e)

            @block.vector
            def _(e):
                for f in p.q["vector"]:
                    f(e)

            @block.scalar
            def _(e):
                for f in p.q["scalar"]:
                    f(e)

            @block.gpsimd
            def _(e):
                for f in p.q["gpsimd"]:
                    f(e)

    return nc


def _stage_inputs(inputs):
    f32 = np.float32
    x = np.asarray(inputs["L0_attn_x"], f32)
    assert np.array_equal(np.asarray(inputs["L0_ffn_gate_g_idx"]),
                          (np.arange(4096) // 128).astype(np.int32))
    assert np.array_equal(np.asarray(inputs["L0_ffn_up_g_idx"]),
                          (np.arange(4096) // 128).astype(np.int32))
    assert np.array_equal(np.asarray(inputs["L0_ffn_down_g_idx"]),
                          (np.arange(11008) // 128).astype(np.int32))

    com = {}
    com["wqT"] = np.ascontiguousarray(np.asarray(inputs["L0_attn_Wq"], f32).T)
    com["wkT"] = np.ascontiguousarray(np.asarray(inputs["L0_attn_Wk"], f32).T)
    com["wvT"] = np.ascontiguousarray(np.asarray(inputs["L0_attn_Wv"], f32).T)
    com["bq"] = np.ascontiguousarray(np.asarray(inputs["L0_attn_bq"], f32).reshape(32, 128).T)
    com["bk"] = np.ascontiguousarray(np.asarray(inputs["L0_attn_bk"], f32).reshape(32, 128).T)
    com["bv"] = np.asarray(inputs["L0_attn_bv"], f32).reshape(1, 4096).astype(bfloat16)
    com["bw"] = np.asarray(inputs["L0_attn_bias_weight"], f32).reshape(1, 48)
    kb = _akron_bases()
    kbs = np.zeros((128, 4 * 2 * 256), dtype=f32)
    for i in range(4):
        for kt2 in range(2):
            kbs[:, (i * 2 + kt2) * 256:(i * 2 + kt2 + 1) * 256] = kb[i][kt2 * 128:(kt2 + 1) * 128, :]
    com["kbasis"] = kbs
    com["onesrow"] = np.ones((1, 128), f32)
    com["onesrow_bf"] = np.ones((1, 128), bfloat16)
    com["onescol_bf"] = np.ones((128, 1), bfloat16)
    com["onescol_f"] = np.ones((128, 1), f32)
    e8 = np.zeros((8, 128), f32)
    for ps in range(128):
        e8[ps // 16, ps] = 1.0
    com["exp8"] = e8
    gn = np.zeros((128, 32 * 32), bfloat16)
    for kt in range(32):
        kbt = kt // 8
        for j in range(128):
            gn[j, kt * 32 + kbt * 8 + j // 16] = 1.0
    com["gones"] = gn
    gi = np.zeros((128, 11 * 86), bfloat16)
    for kb2 in range(11):
        maxp = 128 if kb2 < 10 else 96
        for ps in range(maxp):
            gi[ps, kb2 * 86 + 8 * kb2 + ps // 16] = 1.0
    com["giones"] = gi
    com["qwg"] = np.ascontiguousarray(np.asarray(inputs["L0_ffn_gate_qweight"], np.int32))
    com["qwu"] = np.ascontiguousarray(np.asarray(inputs["L0_ffn_up_qweight"], np.int32))
    com["qwd"] = np.ascontiguousarray(np.asarray(inputs["L0_ffn_down_qweight"], np.int32))
    scg = np.asarray(inputs["L0_ffn_gate_scales"], f32)
    scu = np.asarray(inputs["L0_ffn_up_scales"], f32)
    scd = np.asarray(inputs["L0_ffn_down_scales"], f32)
    com["scd"] = scd
    sgt = np.zeros((8, 4 * INTER), f32)
    sut = np.zeros((8, 4 * INTER), f32)
    for kbt in range(4):
        sgt[:, kbt * INTER:(kbt + 1) * INTER] = scg[kbt * 8:(kbt + 1) * 8, :]
        sut[:, kbt * INTER:(kbt + 1) * INTER] = scu[kbt * 8:(kbt + 1) * 8, :]
    com["scg_t"], com["scu_t"] = sgt, sut
    sdt = np.zeros((8, 11 * D), f32)
    for kb2 in range(11):
        ng = 8 if kb2 < 10 else 6
        sdt[0:ng, kb2 * D:(kb2 + 1) * D] = scd[kb2 * 8:kb2 * 8 + ng, :]
    com["scd_t"] = sdt
    cols = np.zeros(NT_I * 128, dtype=np.int64)
    valid = np.zeros(NT_I * 128, dtype=bool)
    for t in range(NT_I):
        kb2, s2 = divmod(t, 8)
        nj = 128 if kb2 < 10 else 96
        j = np.arange(nj)
        cols[t * 128:t * 128 + nj] = kb2 * 1024 + 8 * j + s2
        valid[t * 128:t * 128 + nj] = True
    scg_p = np.zeros((32, NT_I * 128), f32)
    scu_p = np.zeros((32, NT_I * 128), f32)
    scg_p[:, valid] = scg[:, cols[valid]]
    scu_p[:, valid] = scu[:, cols[valid]]
    com["scg_p"], com["scu_p"] = scg_p, scu_p
    com["qzg"] = np.ascontiguousarray(np.asarray(inputs["L0_ffn_gate_qzeros"], np.int32))
    com["qzu"] = np.ascontiguousarray(np.asarray(inputs["L0_ffn_up_qzeros"], np.int32))
    com["qzd"] = np.ascontiguousarray(np.asarray(inputs["L0_ffn_down_qzeros"], np.int32))

    xT = [np.ascontiguousarray(x[b].T) for b in range(2)]
    in_maps = []
    for core in range(8):
        b, q = core // 4, core % 4
        m = dict(com)
        if q == 0:
            m["xT"] = xT[b]
        else:
            m["xT"] = np.ascontiguousarray(
                np.concatenate([xT[b][:, q * 1024:], xT[b][:, :q * 1024]], axis=1))
        in_maps.append(m)
    return in_maps


def kernel(**inputs):
    in_maps = _stage_inputs(inputs)
    nc = build_program()
    res = run_bass_kernel_spmd(nc, in_maps, core_ids=list(range(8)))
    out = np.empty((2, 4096, 4096), dtype=np.float32)
    for core in range(8):
        b, q = core // 4, core % 4
        out[b, q * 1024:(q + 1) * 1024, :] = np.asarray(res.results[core]["out"]).T
    return out


# version 06 (final)
# speedup vs baseline: 1.2940x; 1.2940x over previous
import numpy as np
import ml_dtypes
from contextlib import ExitStack

import concourse.bass as bass
import concourse.mybir as mybir
from concourse.bass_utils import run_bass_kernel_spmd

F32 = mybir.dt.float32
BF16 = mybir.dt.bfloat16
I32 = mybir.dt.int32
AF = mybir.ActivationFunctionType
ALU = mybir.AluOpType
AX = mybir.AxisListType
bfloat16 = ml_dtypes.bfloat16

D = 4096
L = 4096
OWN = 1024
NH = 16
INTER = 11008
NT_I = 88
EPS = 1e-5


def _ga_tables():
    g = [1.0, 1.0, 1.0]
    n = 8
    masks = sorted(range(n), key=lambda m: (bin(m).count("1"), m))
    inv = {m: i for i, m in enumerate(masks)}
    C = np.zeros((n, n, n), dtype=np.float32)
    for i, a in enumerate(masks):
        for j, b in enumerate(masks):
            sw, at = 0, a >> 1
            while at:
                sw += bin(at & b).count("1")
                at >>= 1
            s = -1.0 if sw & 1 else 1.0
            for t in range(3):
                if a >> t & 1 and b >> t & 1:
                    s *= g[t]
            C[i, j, inv[a ^ b]] = s
    grades = np.array([bin(m).count("1") for m in masks])
    rev = np.where(grades * (grades - 1) // 2 % 2 == 1, -1.0, 1.0).astype(np.float32)
    met = np.ones(8, dtype=np.float32)
    return C[:, :, [4, 5, 6]], rev, met


def _akron_bases():
    biv, rev, met = _ga_tables()
    mats = [(1.0 / 16.0) * np.diag(rev * met)] + [np.diag(rev) @ biv[:, :, c] for c in range(3)]
    I32m = np.eye(32, dtype=np.float32)
    return np.stack([np.kron(I32m, m.astype(np.float32)) for m in mats])


class Rot:

    def __init__(self, bufs):
        self.bufs = bufs
        self.free_tok = [None] * len(bufs)
        self.i = -1

    @property
    def slot(self):
        return self.i % len(self.bufs)

    def next(self):
        self.i += 1
        s = self.i % len(self.bufs)
        return self.bufs[s], self.free_tok[s]

    def done(self, tok):
        self.free_tok[self.i % len(self.bufs)] = tok


class P:
    ENGS = ("sync", "tensor", "vector", "scalar", "gpsimd")

    def __init__(self, nc, sems):
        self.nc = nc
        self.q = {e: [] for e in self.ENGS}
        self.sems = sems
        self.cnt = {name: 0 for name in sems}

    def tok(self, sem):
        return (sem, self.cnt[sem])

    def wait(self, eng, token):
        if token is None:
            return
        if isinstance(token, list):
            for t in token:
                self.wait(eng, t)
            return
        sem, val = token
        if val <= 0:
            return
        s = self.sems[sem]
        self.q[eng].append(lambda e, s=s, v=val: e.wait_ge(s, v))

    def waits(self, eng, tokens):
        seen = {}
        flat = []
        for t in tokens:
            if isinstance(t, list):
                flat.extend(t)
            else:
                flat.append(t)
        for t in flat:
            if t is None:
                continue
            sem, val = t
            if val > seen.get(sem, -1):
                seen[sem] = val
        for sem, val in seen.items():
            self.wait(eng, (sem, val))

    def dma(self, eng, out, in_, sem):
        self.cnt[sem] += 16
        tokv = (sem, self.cnt[sem])
        s = self.sems[sem]
        self.q[eng].append(
            lambda e, o=out, i=in_, s=s: e.dma_start(out=o, in_=i).then_inc(s, 16)
        )
        return tokv

    def op(self, eng, build, inc=None):
        if inc is None:
            self.q[eng].append(build)
            return None
        self.cnt[inc] += 1
        tokv = (inc, self.cnt[inc])
        s = self.sems[inc]
        self.q[eng].append(lambda e, b=build, s=s: b(e).then_inc(s, 1))
        return tokv

    def barrier(self, scratch_row, ones_src):
        dma_toks = [self.tok(n) for n in self.cnt
                    if n not in ("TE", "VE", "SE", "SY") and self.cnt[n] > 0]
        toks = {
            "tensor": self.tok("TE"),
            "vector": self.op("vector", lambda e: e.nop(), inc="VE"),
            "scalar": self.op("scalar", lambda e, d=scratch_row, s=ones_src: e.copy(d, s), inc="SE"),
            "sync": self.op("sync", lambda e: e.nop(), inc="SY"),
        }
        for eng in ("sync", "tensor", "vector", "scalar"):
            self.waits(eng, [t for o, t in toks.items() if o != eng] + dma_toks)


def build_program():
    nc = bass.Bass("TRN2", target_bir_lowering=False, dynamic_dma_scratch_size=2048)

    d_xT = nc.dram_tensor("xT", [D, L], F32, kind="ExternalInput")
    d_wqT = nc.dram_tensor("wqT", [D, D], F32, kind="ExternalInput")
    d_wkT = nc.dram_tensor("wkT", [D, D], F32, kind="ExternalInput")
    d_wvT = nc.dram_tensor("wvT", [D, D], F32, kind="ExternalInput")
    d_bq = nc.dram_tensor("bq", [128, 32], F32, kind="ExternalInput")
    d_bk = nc.dram_tensor("bk", [128, 32], F32, kind="ExternalInput")
    d_bv = nc.dram_tensor("bv", [1, D], BF16, kind="ExternalInput")
    d_bw = nc.dram_tensor("bw", [1, 48], F32, kind="ExternalInput")
    d_kbasis = nc.dram_tensor("kbasis", [128, 4 * 2 * 256], F32, kind="ExternalInput")
    d_onesrow = nc.dram_tensor("onesrow", [1, 128], F32, kind="ExternalInput")
    d_onesrow_bf = nc.dram_tensor("onesrow_bf", [1, 128], BF16, kind="ExternalInput")
    d_onescol_bf = nc.dram_tensor("onescol_bf", [128, 1], BF16, kind="ExternalInput")
    d_onescol_f = nc.dram_tensor("onescol_f", [128, 1], F32, kind="ExternalInput")
    d_gones = nc.dram_tensor("gones", [128, 32 * 32], BF16, kind="ExternalInput")
    d_giones = nc.dram_tensor("giones", [128, 11 * 86], BF16, kind="ExternalInput")
    d_qwg = nc.dram_tensor("qwg", [512, INTER], I32, kind="ExternalInput")
    d_qwu = nc.dram_tensor("qwu", [512, INTER], I32, kind="ExternalInput")
    d_qwd = nc.dram_tensor("qwd", [1376, D], I32, kind="ExternalInput")
    d_sxg = nc.dram_tensor("sxg", [128, 4 * INTER], F32, kind="ExternalInput")
    d_sxu = nc.dram_tensor("sxu", [128, 4 * INTER], F32, kind="ExternalInput")
    d_sxd = nc.dram_tensor("sxd", [128, 11 * D], F32, kind="ExternalInput")
    d_scg_p = nc.dram_tensor("scg_p", [32, NT_I * 128], F32, kind="ExternalInput")
    d_scu_p = nc.dram_tensor("scu_p", [32, NT_I * 128], F32, kind="ExternalInput")
    d_scd = nc.dram_tensor("scd", [86, D], F32, kind="ExternalInput")
    d_qzg = nc.dram_tensor("qzg", [32, 1376], I32, kind="ExternalInput")
    d_qzu = nc.dram_tensor("qzu", [32, 1376], I32, kind="ExternalInput")
    d_qzd = nc.dram_tensor("qzd", [86, 512], I32, kind="ExternalInput")
    d_out = nc.dram_tensor("out", [D, OWN], F32, kind="ExternalOutput")

    s_xln = nc.dram_tensor("s_xln", [D, L], BF16)
    s_kT = nc.dram_tensor("s_kT", [D, L], BF16)
    s_v = nc.dram_tensor("s_v", [L, D], BF16)
    s_qT = nc.dram_tensor("s_qT", [D, OWN], BF16)
    s_attn = nc.dram_tensor("s_attn", [D, OWN], F32)
    s_x2 = nc.dram_tensor("s_x2", [D, OWN], F32)
    s_xln2 = nc.dram_tensor("s_xln2", [D, OWN], BF16)
    s_hT = nc.dram_tensor("s_hT", [NT_I * 128, OWN], BF16)
    s_m2g = nc.dram_tensor("s_m2g", [32, NT_I * 128], F32)
    s_m2u = nc.dram_tensor("s_m2u", [32, NT_I * 128], F32)

    es = ExitStack()
    with es:
        sems = {}
        semnames = ["LD", "ST", "TE", "VE", "SE", "SY", "XL", "HL", "I2", "I3"]
        for t in ("A", "B", "C", "D", "E", "F", "G", "H", "I", "J", "M", "K", "V"):
            semnames += [t + "0", t + "1"]
        for name in semnames:
            sems[name] = es.enter_context(nc.semaphore(name))
        p = P(nc, sems)

        ones_bf = es.enter_context(nc.sbuf_tensor("ones_bf", [128, 1], BF16))
        ones_f = es.enter_context(nc.sbuf_tensor("ones_f", [128, 1], F32))
        onesrow = es.enter_context(nc.sbuf_tensor("sb_onesrow", [1, 128], F32))
        onesrow_bf = es.enter_context(nc.sbuf_tensor("sb_onesrow_bf", [1, 128], BF16))
        bq_sb = es.enter_context(nc.sbuf_tensor("bq_sb", [128, 32], F32))
        bk_sb = es.enter_context(nc.sbuf_tensor("bk_sb", [128, 32], F32))
        hg_sb = es.enter_context(nc.sbuf_tensor("hg_sb", [86, OWN], BF16))
        scrw = es.enter_context(nc.sbuf_tensor("scrw", [1, 4], F32))

        def barrier():
            p.barrier(scrw[:1, 0:1], ones_f[:1, 0:1])

        for dst, src in ((ones_bf, d_onescol_bf), (ones_f, d_onescol_f),
                         (onesrow, d_onesrow), (onesrow_bf, d_onesrow_bf),
                         (bq_sb, d_bq), (bk_sb, d_bk)):
            t_c = p.dma("sync", dst[:, :], src[:, :], "LD")

        p.waits("tensor", [t_c])
        barrier()

        CH = 256
        with ExitStack() as ph:
            xf = Rot([ph.enter_context(nc.sbuf_tensor(f"xf{i}", [128, 32 * CH], F32)) for i in range(2)])
            xbf = ph.enter_context(nc.sbuf_tensor("xbf", [128, 32 * CH], BF16))
            xsq = Rot([ph.enter_context(nc.sbuf_tensor(f"xsq{i}", [128, CH], BF16)) for i in range(2)])
            rows = ph.enter_context(nc.sbuf_tensor("rows", [1, 6 * CH], F32))
            rbs = ph.enter_context(nc.sbuf_tensor("rbs", [128, CH], F32))
            mbs = ph.enter_context(nc.sbuf_tensor("mbs", [128, CH], F32))
            tmpt = ph.enter_context(nc.sbuf_tensor("tmpt", [128, CH], F32))
            xlnb = ph.enter_context(nc.sbuf_tensor("xlnb", [128, 32 * CH], BF16))
            p_s1 = ph.enter_context(nc.psum_tensor("p_s1", [1, CH], F32))
            p_s2 = ph.enter_context(nc.psum_tensor("p_s2", [1, CH], F32))
            p_b1 = ph.enter_context(nc.psum_tensor("p_b1", [128, CH], F32))
            p_b2 = ph.enter_context(nc.psum_tensor("p_b2", [128, CH], F32))

            t_store = None
            t_pb_read = None
            t_ps_read = None
            for mc in range(16):
                col0 = mc * CH
                xfb, war_xf = xf.next()
                t_cast = []
                for kg in range(4):
                    p.waits("sync", [war_xf if kg == 0 else None])
                    ap = bass.AP(d_xT, (kg * 8 * 128) * L + col0,
                                 [[L, 128], [128 * L, 8], [1, CH]])
                    tl = p.dma("sync", xfb[:, kg * 8 * CH:(kg + 1) * 8 * CH], ap, "A%d" % xf.slot)
                    p.waits("vector", [tl, war_xf if kg == 0 else None])
                    tc_ = p.op("vector", lambda e, b=xfb, kg=kg: e.tensor_copy(
                        xbf[:, kg * 8 * CH:(kg + 1) * 8 * CH],
                        b[:, kg * 8 * CH:(kg + 1) * 8 * CH]), inc="VE")
                    t_cast.append(tc_)
                p.waits("tensor", [t_ps_read])
                t_sx = None
                for kt in range(32):
                    if kt % 8 == 0:
                        p.wait("tensor", t_cast[kt // 8])
                    sl = xbf[:, kt * CH:(kt + 1) * CH]
                    t_sx = p.op("tensor", lambda e, sl=sl, kt=kt: e.matmul(
                        p_s1[:1, :], ones_bf[:, :1], sl, start=(kt == 0), stop=(kt == 31)),
                        inc="TE" if kt == 31 else None)
                t_sq = None
                for kt in range(32):
                    sl = xbf[:, kt * CH:(kt + 1) * CH]
                    sq, war = xsq.next()
                    p.waits("vector", [war, t_cast[kt // 8]])
                    tv = p.op("vector", lambda e, sl=sl, sq=sq: e.tensor_mul(sq[:, :], sl, sl), inc="VE")
                    p.wait("tensor", tv)
                    t_sq = p.op("tensor", lambda e, kt=kt, sq=sq: e.matmul(
                        p_s2[:1, :], ones_bf[:, :1], sq[:, :], start=(kt == 0), stop=(kt == 31)), inc="TE")
                    xsq.done(t_sq)
                mrow, e2r, var, rstd, nmr = (rows[:1, i * CH:(i + 1) * CH] for i in range(5))
                p.waits("vector", [t_sx, t_sq])
                p.op("vector", lambda e, mrow=mrow: e.tensor_scalar_mul(mrow, p_s1[:1, :], 1.0 / D))
                t_ps_read = p.op("vector", lambda e, e2r=e2r: e.tensor_scalar_mul(
                    e2r, p_s2[:1, :], 1.0 / D), inc="VE")
                p.op("vector", lambda e, var=var, mrow=mrow: e.tensor_mul(var, mrow, mrow))
                p.op("vector", lambda e, var=var, e2r=e2r: e.tensor_sub(var, e2r, var))
                t_vr = p.op("vector", lambda e, var=var: e.tensor_scalar_add(var, var, EPS), inc="VE")
                p.wait("scalar", t_vr)
                p.op("scalar", lambda e, rstd=rstd, var=var: e.activation(rstd, var, AF.Ln))
                t_ex = p.op("scalar", lambda e, rstd=rstd: e.activation(
                    rstd, rstd, AF.Exp, scale=-0.5), inc="SE")
                p.wait("vector", t_ex)
                t_nm = p.op("vector", lambda e, nmr=nmr, mrow=mrow, rstd=rstd: e.scalar_tensor_tensor(
                    nmr, mrow, -1.0, rstd, ALU.mult, ALU.mult), inc="VE")
                p.waits("tensor", [t_ex, t_nm, t_pb_read])
                tb1 = p.op("tensor", lambda e, rstd=rstd: e.matmul(
                    p_b1[:, :], onesrow[:1, :], rstd, start=True, stop=True), inc="TE")
                tb2 = p.op("tensor", lambda e, nmr=nmr: e.matmul(
                    p_b2[:, :], onesrow[:1, :], nmr, start=True, stop=True), inc="TE")
                p.waits("vector", [tb1, tb2])
                p.op("vector", lambda e: e.tensor_copy(rbs[:, :], p_b1[:, :]))
                t_pb_read = p.op("vector", lambda e: e.tensor_copy(mbs[:, :], p_b2[:, :]), inc="VE")
                p.wait("vector", t_store)
                tva = None
                for kt in range(32):
                    sl = xfb[:, kt * CH:(kt + 1) * CH]
                    dsl = xlnb[:, kt * CH:(kt + 1) * CH]
                    p.op("vector", lambda e, sl=sl: e.tensor_mul(tmpt[:, :], sl, rbs[:, :]))
                    tva = p.op("vector", lambda e, dsl=dsl: e.tensor_add(dsl, tmpt[:, :], mbs[:, :]),
                               inc="VE")
                xf.done(tva)
                p.wait("scalar", tva)
                apo = bass.AP(s_xln, col0, [[L, 128], [128 * L, 32], [1, CH]])
                t_store = p.dma("scalar", apo, xlnb[:, :], "ST")
            barrier()

        def emit_proj(wT_dram, bias_sb, out_dram, out_cols, m_chunks, tag):
            with ExitStack() as ph:
                wf = Rot([ph.enter_context(nc.sbuf_tensor(f"{tag}wf{i}", [128, 8 * 512], F32)) for i in range(2)])
                wbf = Rot([ph.enter_context(nc.sbuf_tensor(f"{tag}wbf{i}", [128, 32 * 512], BF16)) for i in range(2)])
                xl = Rot([ph.enter_context(nc.sbuf_tensor(f"{tag}xl{i}", [128, 32 * 512], BF16)) for i in range(2)])
                ev = Rot([ph.enter_context(nc.sbuf_tensor(f"{tag}ev{i}", [128, 512], BF16)) for i in range(2)])
                pp = Rot([ph.enter_context(nc.psum_tensor(f"{tag}pp{i}", [128, 512], F32)) for i in range(2)])
                for cb in range(8):
                    wb, war_w = wbf.next()
                    wcast = None
                    for kg in range(4):
                        buf, war = wf.next()
                        p.waits("sync", [war, war_w if kg == 0 else None])
                        ap = bass.AP(wT_dram, (kg * 8 * 128) * D + cb * 512,
                                     [[D, 128], [128 * D, 8], [1, 512]])
                        tl = p.dma("sync", buf[:, :], ap, "B%d" % wf.slot)
                        p.waits("vector", [tl, war_w if kg == 0 else None])
                        wcast = p.op("vector", lambda e, b=buf, wb=wb, kg=kg: e.tensor_copy(
                            wb[:, kg * 8 * 512:(kg + 1) * 8 * 512], b[:, :]), inc="VE")
                        wf.done(wcast)
                    t_mm_last = None
                    for mb in m_chunks:
                        xb, war_x = xl.next()
                        p.wait("sync", war_x)
                        apx = bass.AP(s_xln, mb * 512, [[L, 128], [128 * L, 32], [1, 512]])
                        tx = p.dma("sync", xb[:, :], apx, "C%d" % xl.slot)
                        p.waits("tensor", [tx, wcast])
                        for ct in range(4):
                            ps, war_p = pp.next()
                            p.wait("tensor", war_p)
                            tmm = None
                            for kt in range(32):
                                tmm = p.op("tensor", lambda e, ps=ps, wb=wb, kt=kt, ct=ct, xb=xb: e.matmul(
                                    ps[:, :],
                                    wb[:, kt * 512 + ct * 128:kt * 512 + ct * 128 + 128],
                                    xb[:, kt * 512:(kt + 1) * 512],
                                    start=(kt == 0), stop=(kt == 31)),
                                    inc="TE" if kt == 31 else None)
                            t_mm_last = tmm
                            eb, war_e = ev.next()
                            p.waits("vector", [tmm, war_e])
                            bcol = bias_sb[:, cb * 4 + ct:cb * 4 + ct + 1]
                            tevd = p.op("vector", lambda e, eb=eb, ps=ps, bcol=bcol: e.tensor_scalar_add(
                                eb[:, :], ps[:, :], bcol), inc="VE")
                            pp.done(tevd)
                            p.wait("scalar", tevd)
                            row0 = cb * 512 + ct * 128
                            apo = bass.AP(out_dram, row0 * out_cols + (mb - m_chunks[0]) * 512,
                                          [[out_cols, 128], [1, 512]])
                            tst = p.dma("scalar", apo, eb[:, :], "ST")
                            ev.done(tst)
                        xl.done(t_mm_last)
                    wbf.done(t_mm_last)
            barrier()

        emit_proj(d_wkT, bk_sb, s_kT, L, list(range(8)), "k")
        emit_proj(d_wqT, bq_sb, s_qT, OWN, [0, 1], "q")

        with ExitStack() as ph:
            bv_sb = ph.enter_context(nc.sbuf_tensor("bv_sb", [1, D], BF16))
            t_bv = p.dma("sync", bv_sb[:, :], d_bv[:, :], "LD")
            wf = Rot([ph.enter_context(nc.sbuf_tensor(f"vwf{i}", [128, 8 * 512], F32)) for i in range(2)])
            wbf = Rot([ph.enter_context(nc.sbuf_tensor(f"vwbf{i}", [128, 32 * 512], BF16)) for i in range(2)])
            xl = Rot([ph.enter_context(nc.sbuf_tensor(f"vxl{i}", [128, 32 * 128], BF16)) for i in range(2)])
            ev = Rot([ph.enter_context(nc.sbuf_tensor(f"vev{i}", [128, 512], BF16)) for i in range(2)])
            pp = Rot([ph.enter_context(nc.psum_tensor(f"vpp{i}", [128, 512], F32)) for i in range(2)])
            for cb in range(8):
                wb, war_w = wbf.next()
                wcast = None
                for kg in range(4):
                    buf, war = wf.next()
                    p.waits("sync", [war, war_w if kg == 0 else None])
                    ap = bass.AP(d_wvT, (kg * 8 * 128) * D + cb * 512,
                                 [[D, 128], [128 * D, 8], [1, 512]])
                    tl = p.dma("sync", buf[:, :], ap, "B%d" % wf.slot)
                    p.waits("vector", [tl, war_w if kg == 0 else None])
                    wcast = p.op("vector", lambda e, b=buf, wb=wb, kg=kg: e.tensor_copy(
                        wb[:, kg * 8 * 512:(kg + 1) * 8 * 512], b[:, :]), inc="VE")
                    wf.done(wcast)
                t_last = None
                for mt in range(32):
                    xb, war_x = xl.next()
                    p.wait("sync", war_x)
                    apx = bass.AP(s_xln, mt * 128, [[L, 128], [128 * L, 32], [1, 128]])
                    tx = p.dma("sync", xb[:, :], apx, "C%d" % xl.slot)
                    ps, war_p = pp.next()
                    p.waits("tensor", [tx, wcast, war_p])
                    for kt in range(32):
                        p.op("tensor", lambda e, ps=ps, xb=xb, kt=kt, wb=wb: e.matmul(
                            ps[:, :],
                            xb[:, kt * 128:(kt + 1) * 128],
                            wb[:, kt * 512:(kt + 1) * 512],
                            start=(kt == 0), stop=False))
                    if cb == 0 and mt == 0:
                        p.wait("tensor", t_bv)
                    tmm = p.op("tensor", lambda e, ps=ps, cb=cb: e.matmul(
                        ps[:, :], onesrow_bf[:1, :], bv_sb[:1, cb * 512:(cb + 1) * 512],
                        start=False, stop=True), inc="TE")
                    t_last = tmm
                    xl.done(tmm)
                    eb, war_e = ev.next()
                    p.waits("vector", [tmm, war_e])
                    tevd = p.op("vector", lambda e, eb=eb, ps=ps: e.tensor_copy(eb[:, :], ps[:, :]), inc="VE")
                    pp.done(tevd)
                    p.wait("scalar", tevd)
                    apo = bass.AP(s_v, (mt * 128) * D + cb * 512, [[D, 128], [1, 512]])
                    tst = p.dma("scalar", apo, eb[:, :], "ST")
                    ev.done(tst)
                wbf.done(t_last)
        barrier()

        q_es = ExitStack()
        akron = q_es.enter_context(nc.sbuf_tensor("akron", [128, NH * 2 * 256], BF16))
        qeff = q_es.enter_context(nc.sbuf_tensor("qeff", [128, 32 * OWN], BF16))
        with nc.sbuf_tensor("sb_kbasis", [128, 4 * 2 * 256], F32) as kbasis, \
             nc.sbuf_tensor("bwrow", [1, 48], F32) as bwrow, \
             nc.sbuf_tensor("bwcast", [128, 48], F32) as bwcast, \
             nc.sbuf_tensor("ak_acc", [128, 256], F32) as ak_acc, \
             nc.psum_tensor("p_bw", [128, 48], F32) as p_bw:
            tkb = p.dma("sync", kbasis[:, :], d_kbasis[:, :], "LD")
            tbw = p.dma("sync", bwrow[:, :], d_bw[:, :], "LD")
            p.waits("tensor", [tkb, tbw])
            t_mm = p.op("tensor", lambda e: e.matmul(
                p_bw[:, :], onesrow[:1, :], bwrow[:1, :], start=True, stop=True), inc="TE")
            p.wait("vector", t_mm)
            p.op("vector", lambda e: e.tensor_copy(bwcast[:, :], p_bw[:, :]), inc="VE")
            for h in range(NH):
                for kt2 in range(2):
                    dst = akron[:, (h * 2 + kt2) * 256:(h * 2 + kt2 + 1) * 256]
                    sl = [kbasis[:, (i * 2 + kt2) * 256:(i * 2 + kt2 + 1) * 256] for i in range(4)]
                    sc = [bwcast[:, h * 3 + c:h * 3 + c + 1] for c in range(3)]
                    p.op("vector", lambda e, a=sl[1], s=sc[0], b=sl[0]: e.scalar_tensor_tensor(
                        ak_acc[:, :], a, s, b, ALU.mult, ALU.add))
                    p.op("vector", lambda e, a=sl[2], s=sc[1]: e.scalar_tensor_tensor(
                        ak_acc[:, :], a, s, ak_acc[:, :], ALU.mult, ALU.add))
                    t_ak = p.op("vector", lambda e, a=sl[3], s=sc[2], d=dst: e.scalar_tensor_tensor(
                        d, a, s, ak_acc[:, :], ALU.mult, ALU.add), inc="VE")
            p.wait("tensor", t_ak)
        barrier()
        with ExitStack() as ph:
            qh = Rot([ph.enter_context(nc.sbuf_tensor(f"qh{i}", [128, 2 * OWN], BF16)) for i in range(2)])
            pq = Rot([ph.enter_context(nc.psum_tensor(f"pq{i}", [128, 512], F32)) for i in range(2)])
            for h in range(NH):
                qb, war_q = qh.next()
                p.wait("sync", war_q)
                apq = bass.AP(s_qT, (h * 2 * 128) * OWN, [[OWN, 128], [128 * OWN, 2], [1, OWN]])
                tq = p.dma("sync", qb[:, :], apq, "D%d" % qh.slot)
                t_last = None
                for ct2 in range(2):
                    for lc in range(2):
                        ps, war_p = pq.next()
                        p.waits("tensor", [tq, war_p])
                        tmm = None
                        for kt2 in range(2):
                            tmm = p.op("tensor", lambda e, ps=ps, h=h, kt2=kt2, ct2=ct2, lc=lc, qb=qb: e.matmul(
                                ps[:, :],
                                akron[:, (h * 2 + kt2) * 256 + ct2 * 128:(h * 2 + kt2) * 256 + ct2 * 128 + 128],
                                qb[:, kt2 * OWN + lc * 512:kt2 * OWN + lc * 512 + 512],
                                start=(kt2 == 0), stop=(kt2 == 1)), inc="TE")
                        p.wait("vector", tmm)
                        dst = qeff[:, (h * 2 + ct2) * OWN + lc * 512:(h * 2 + ct2) * OWN + lc * 512 + 512]
                        tev = p.op("vector", lambda e, dst=dst, ps=ps: e.tensor_copy(dst, ps[:, :]), inc="VE")
                        pq.done(tev)
                        t_last = tev
                qh.done(t_last)
        barrier()

        with ExitStack() as ph:
            kh = Rot([ph.enter_context(nc.sbuf_tensor(f"kh{i}", [128, 2 * L], BF16)) for i in range(2)])
            vh = Rot([ph.enter_context(nc.sbuf_tensor(f"vh{i}", [128, 32 * 256], BF16)) for i in range(2)])
            Eb = ph.enter_context(nc.sbuf_tensor("Eb", [128, 32 * 512], BF16))
            Sacc = ph.enter_context(nc.sbuf_tensor("Sacc", [128, 512], F32))
            ksq = Rot([ph.enter_context(nc.sbuf_tensor(f"ksq{i}", [128, 512], BF16)) for i in range(2)])
            smax = ph.enter_context(nc.sbuf_tensor("smax", [1, 8], F32))
            bnd = ph.enter_context(nc.sbuf_tensor("bnd", [1, 96], F32))
            rrow = ph.enter_context(nc.sbuf_tensor("rrow", [1, 512], F32))
            negb_sb = ph.enter_context(nc.sbuf_tensor("negb_sb", [128, 32], F32))
            rb_sb = ph.enter_context(nc.sbuf_tensor("rb_sb", [128, 512], F32))
            onorm = Rot([ph.enter_context(nc.sbuf_tensor(f"onorm{i}", [128, 512], F32)) for i in range(2)])
            p_log = Rot([ph.enter_context(nc.psum_tensor(f"p_log{i}", [128, 512], F32)) for i in range(3)])
            p_out = [ph.enter_context(nc.psum_tensor(f"p_out{i}", [128, 512], F32)) for i in range(2)]
            p_row = Rot([ph.enter_context(nc.psum_tensor(f"p_row{i}", [1, 512], F32)) for i in range(2)])
            p_rb = ph.enter_context(nc.psum_tensor("p_rb", [128, 512], F32))

            t_k = {}
            t_kv = {}
            t_kread_done = {}

            def load_kv(h):
                if h >= NH:
                    return
                kb, war1 = kh.next()
                vb, war2 = vh.next()
                p.waits("sync", [war1, war2, t_kread_done.get(h - 2)])
                apk = bass.AP(s_kT, (h * 2 * 128) * L, [[L, 128], [128 * L, 2], [1, L]])
                t_k[h] = p.dma("sync", kb[:, :], apk, "K%d" % kh.slot)
                apv = bass.AP(s_v, h * 256, [[D, 128], [128 * D, 32], [1, 256]])
                t_kv[h] = p.dma("sync", vb[:, :], apv, "V%d" % vh.slot)
                return kb, vb

            st3 = {"t_p_rb_free": None, "t_lnk": None, "t_comb": None, "t_bc": None,
                   "t_lnq": None}
            t_nb = {}

            t_qm = None
            for h in range(NH):
                for lc in range(2):
                    tq = None
                    for kt2 in range(2):
                        sl = qeff[:, (h * 2 + kt2) * OWN + lc * 512:(h * 2 + kt2) * OWN + lc * 512 + 512]
                        sq, war = ksq.next()
                        p.wait("vector", war)
                        tv = p.op("vector", lambda e, sl=sl, sq=sq: e.tensor_mul(sq[:, :], sl, sl), inc="VE")
                        if kt2 == 0:
                            pr, war_r = p_row.next()
                        p.waits("tensor", [tv, war_r if kt2 == 0 else None])
                        tq = p.op("tensor", lambda e, kt2=kt2, sq=sq, pr=pr: e.matmul(
                            pr[:1, :], ones_bf[:, :1], sq[:, :],
                            start=(kt2 == 0), stop=(kt2 == 1)), inc="TE")
                        ksq.done(tq)
                    p.wait("vector", tq)
                    dst = bnd[:1, 16 + h * 2 + lc:16 + h * 2 + lc + 1]
                    t_qm = p.op("vector", lambda e, dst=dst, pr=pr: e.reduce_max(dst, pr[:1, :], axis=AX.X), inc="VE")
                    p_row.done(t_qm)
            p.wait("scalar", t_qm)
            st3["t_lnq"] = p.op("scalar", lambda e: e.activation(
                bnd[:1, 48:80], bnd[:1, 16:48], AF.Ln), inc="SE")

            def emit_kbound(h, kb):
                p.wait("vector", st3["t_lnk"])
                t_r = None
                for mc8 in range(8):
                    tq = None
                    for kt2 in range(2):
                        sl = kb[:, kt2 * L + mc8 * 512:kt2 * L + mc8 * 512 + 512]
                        sq, war = ksq.next()
                        p.waits("vector", [war, t_k[h] if mc8 == 0 and kt2 == 0 else None])
                        tv = p.op("vector", lambda e, sl=sl, sq=sq: e.tensor_mul(sq[:, :], sl, sl), inc="VE")
                        if kt2 == 0:
                            pr, war_r = p_row.next()
                        p.waits("tensor", [tv, war_r if kt2 == 0 else None])
                        tq = p.op("tensor", lambda e, kt2=kt2, sq=sq, pr=pr: e.matmul(
                            pr[:1, :], ones_bf[:, :1], sq[:, :],
                            start=(kt2 == 0), stop=(kt2 == 1)), inc="TE")
                        ksq.done(tq)
                    p.wait("vector", tq)
                    dst = smax[:1, 0:1] if mc8 == 0 else smax[:1, 1:2]
                    t_r = p.op("vector", lambda e, dst=dst, pr=pr: e.reduce_max(dst, pr[:1, :], axis=AX.X), inc="VE")
                    p_row.done(t_r)
                    if mc8 > 0:
                        t_r = p.op("vector", lambda e: e.tensor_max(
                            smax[:1, 0:1], smax[:1, 0:1], smax[:1, 1:2]), inc="VE")
                p.wait("scalar", t_r)
                t_lnk = p.op("scalar", lambda e: e.activation(smax[:1, 2:3], smax[:1, 0:1], AF.Ln), inc="SE")
                st3["t_lnk"] = t_lnk
                p.waits("vector", [t_lnk, st3["t_lnq"], st3["t_comb"]])
                t_s = p.op("vector", lambda e, h=h: e.tensor_scalar_add(
                    bnd[:1, 80:82], bnd[:1, 48 + 2 * h:48 + 2 * h + 2], smax[:1, 2:3]), inc="VE")
                p.wait("scalar", t_s)
                t_e = p.op("scalar", lambda e: e.activation(
                    bnd[:1, 82:84], bnd[:1, 80:82], AF.Exp, scale=0.5), inc="SE")
                st3["t_comb"] = t_e
                p.waits("vector", [t_e, st3["t_bc"]])
                t_n = p.op("vector", lambda e: e.tensor_scalar_mul(
                    bnd[:1, 84:86], bnd[:1, 82:84], -1.02), inc="VE")
                p.waits("tensor", [t_n, st3["t_p_rb_free"]])
                t_bc = p.op("tensor", lambda e: e.matmul(
                    p_rb[:, 0:2], onesrow[:1, :], bnd[:1, 84:86], start=True, stop=True), inc="TE")
                st3["t_bc"] = t_bc
                p.wait("vector", t_bc)
                t_nb[h] = p.op("vector", lambda e, h=h: e.tensor_copy(
                    negb_sb[:, 2 * h:2 * h + 2], p_rb[:, 0:2]), inc="VE")
                st3["t_p_rb_free"] = t_nb[h]

            cur = load_kv(0)
            emit_kbound(0, cur[0])
            t_out_free = [None, None]
            t_av_prev = {}
            t_sacc_prev = {}
            t_sr_prev = None
            t_logfree = [None, None, None]
            for h in range(NH):
                kb, vb = cur
                cur = load_kv(h + 1)
                for lc in range(2):
                    t_exp = {}
                    t_sacc = {}
                    t_av = {}
                    for mt in range(34):
                        if mt < 32:
                            pl = p_log.bufs[mt % 3]
                            p.waits("tensor", [t_logfree[mt % 3],
                                               t_k[h] if lc == 0 and mt == 0 else None])
                            tl = None
                            for kt2 in range(2):
                                tl = p.op("tensor", lambda e, pl=pl, kb=kb, kt2=kt2, mt=mt, h=h, lc=lc: e.matmul(
                                    pl[:, :],
                                    kb[:, kt2 * L + mt * 128:kt2 * L + mt * 128 + 128],
                                    qeff[:, (h * 2 + kt2) * OWN + lc * 512:(h * 2 + kt2) * OWN + lc * 512 + 512],
                                    start=(kt2 == 0), stop=(kt2 == 1)),
                                    inc="TE" if kt2 == 1 else None)
                            p.waits("scalar", [tl,
                                               t_nb[h] if mt == 0 and lc == 0 else None,
                                               t_av_prev.get(mt), t_sacc_prev.get(mt)])
                            t_exp[mt] = p.op("scalar", lambda e, pl=pl, mt=mt, h=h, lc=lc: e.activation(
                                Eb[:, mt * 512:(mt + 1) * 512], pl[:, :], AF.Exp,
                                bias=negb_sb[:, 2 * h + lc:2 * h + lc + 1]), inc="SE")
                            t_logfree[mt % 3] = t_exp[mt]
                            p.waits("vector", [t_exp[mt], t_sr_prev if mt == 0 else None])
                            if mt == 0:
                                t_sacc[mt] = p.op("vector", lambda e: e.tensor_copy(
                                    Sacc[:, :], Eb[:, 0:512]), inc="VE")
                            else:
                                t_sacc[mt] = p.op("vector", lambda e, mt=mt: e.tensor_add(
                                    Sacc[:, :], Sacc[:, :], Eb[:, mt * 512:(mt + 1) * 512]), inc="VE")
                        if mt >= 2:
                            m = mt - 2
                            p.waits("tensor", [t_exp[m]] +
                                    ([t_out_free[0], t_out_free[1],
                                      t_kv[h] if lc == 0 else None] if m == 0 else []))
                            for ct in range(2):
                                t_av[m] = p.op("tensor", lambda e, po=p_out[ct], vb=vb, m=m, ct=ct: e.matmul(
                                    po[:, :],
                                    vb[:, m * 256 + ct * 128:m * 256 + ct * 128 + 128],
                                    Eb[:, m * 512:(m + 1) * 512],
                                    start=(m == 0), stop=(m == 31)),
                                    inc="TE" if ct == 1 else None)
                    if lc == 0 and h + 1 < NH:
                        emit_kbound(h + 1, cur[0])
                    pr, war_r = p_row.next()
                    p.waits("tensor", [t_sacc[31], war_r])
                    t_sr = p.op("tensor", lambda e, pr=pr: e.matmul(
                        pr[:1, :], ones_f[:, :1], Sacc[:, :], start=True, stop=True), inc="TE")
                    t_sr_prev = t_sr
                    p.wait("vector", t_sr)
                    t_rc = p.op("vector", lambda e, pr=pr: e.reciprocal(rrow[:1, :], pr[:1, :]), inc="VE")
                    p_row.done(t_rc)
                    p.waits("tensor", [t_rc, st3["t_p_rb_free"]])
                    t_rb = p.op("tensor", lambda e: e.matmul(
                        p_rb[:, :], onesrow[:1, :], rrow[:1, :], start=True, stop=True), inc="TE")
                    st3["t_bc"] = t_rb
                    p.wait("vector", t_rb)
                    t_rbc = p.op("vector", lambda e: e.tensor_copy(rb_sb[:, :], p_rb[:, :]), inc="VE")
                    st3["t_p_rb_free"] = t_rbc
                    for ct in range(2):
                        ob, war_o = onorm.next()
                        p.waits("vector", [t_av[31], t_rbc, war_o])
                        tno = p.op("vector", lambda e, ob=ob, po=p_out[ct]: e.tensor_mul(
                            ob[:, :], po[:, :], rb_sb[:, :]), inc="VE")
                        t_out_free[ct] = tno
                        p.wait("scalar", tno)
                        row0 = h * 256 + ct * 128
                        apo = bass.AP(s_attn, row0 * OWN + lc * 512, [[OWN, 128], [1, 512]])
                        tst = p.dma("scalar", apo, ob[:, :], "ST")
                        onorm.done(tst)
                    t_av_prev = t_av
                    t_sacc_prev = t_sacc
                t_kread_done[h] = t_av[31]
        q_es.close()
        barrier()

        CH4 = 256
        with ExitStack() as ph:
            ain = Rot([ph.enter_context(nc.sbuf_tensor(f"ain{i}", [128, 8 * CH4], F32)) for i in range(2)])
            bin_ = Rot([ph.enter_context(nc.sbuf_tensor(f"bin{i}", [128, 8 * CH4], F32)) for i in range(2)])
            x2f = ph.enter_context(nc.sbuf_tensor("x2f", [128, 32 * CH4], F32))
            x2b = ph.enter_context(nc.sbuf_tensor("x2b", [128, 32 * CH4], BF16))
            xsq = Rot([ph.enter_context(nc.sbuf_tensor(f"xsq4{i}", [128, CH4], BF16)) for i in range(2)])
            rows = ph.enter_context(nc.sbuf_tensor("rows4", [1, 6 * CH4], F32))
            rbs = ph.enter_context(nc.sbuf_tensor("rbs4", [128, CH4], F32))
            mbs = ph.enter_context(nc.sbuf_tensor("mbs4", [128, CH4], F32))
            tmpt = ph.enter_context(nc.sbuf_tensor("tmpt4", [128, CH4], F32))
            xlnb = ph.enter_context(nc.sbuf_tensor("xlnb4", [128, 32 * CH4], BF16))
            p_s1 = ph.enter_context(nc.psum_tensor("p4_s1", [1, CH4], F32))
            p_s2 = ph.enter_context(nc.psum_tensor("p4_s2", [1, CH4], F32))
            p_b1 = ph.enter_context(nc.psum_tensor("p4_b1", [128, CH4], F32))
            p_b2 = ph.enter_context(nc.psum_tensor("p4_b2", [128, CH4], F32))
            t_store = [None, None]
            t_pb_read = None
            t_ps_read = None
            for mc in range(4):
                t_add = []
                for kg in range(4):
                    ab, wa = ain.next()
                    bb, wb_ = bin_.next()
                    p.waits("sync", [wa, wb_])
                    apa = bass.AP(d_xT, (kg * 8 * 128) * L + mc * CH4,
                                  [[L, 128], [128 * L, 8], [1, CH4]])
                    ta = p.dma("sync", ab[:, :], apa, "E%d" % ain.slot)
                    apb = bass.AP(s_attn, (kg * 8 * 128) * OWN + mc * CH4,
                                  [[OWN, 128], [128 * OWN, 8], [1, CH4]])
                    tb = p.dma("sync", bb[:, :], apb, "F%d" % bin_.slot)
                    p.waits("vector", [ta, tb, t_store[0] if kg == 0 else None])
                    tx2 = p.op("vector", lambda e, ab=ab, bb=bb, kg=kg: e.tensor_add(
                        x2f[:, kg * 8 * CH4:(kg + 1) * 8 * CH4], ab[:, :], bb[:, :]), inc="VE")
                    ain.done(tx2)
                    bin_.done(tx2)
                    tcb = p.op("vector", lambda e, kg=kg: e.tensor_copy(
                        x2b[:, kg * 8 * CH4:(kg + 1) * 8 * CH4],
                        x2f[:, kg * 8 * CH4:(kg + 1) * 8 * CH4]), inc="VE")
                    t_add.append(tcb)
                p.wait("scalar", t_add[3])
                apx2 = bass.AP(s_x2, mc * CH4, [[OWN, 128], [128 * OWN, 32], [1, CH4]])
                t_store[0] = p.dma("scalar", apx2, x2f[:, :], "ST")
                p.waits("tensor", [t_ps_read])
                t_sx = None
                for kt in range(32):
                    if kt % 8 == 0:
                        p.wait("tensor", t_add[kt // 8])
                    sl = x2b[:, kt * CH4:(kt + 1) * CH4]
                    t_sx = p.op("tensor", lambda e, sl=sl, kt=kt: e.matmul(
                        p_s1[:1, :], ones_bf[:, :1], sl, start=(kt == 0), stop=(kt == 31)),
                        inc="TE" if kt == 31 else None)
                t_sq = None
                for kt in range(32):
                    sl = x2b[:, kt * CH4:(kt + 1) * CH4]
                    sq, war = xsq.next()
                    p.wait("vector", war)
                    tv = p.op("vector", lambda e, sl=sl, sq=sq: e.tensor_mul(sq[:, :], sl, sl), inc="VE")
                    p.wait("tensor", tv)
                    t_sq = p.op("tensor", lambda e, kt=kt, sq=sq: e.matmul(
                        p_s2[:1, :], ones_bf[:, :1], sq[:, :], start=(kt == 0), stop=(kt == 31)), inc="TE")
                    xsq.done(t_sq)
                mrow, e2r, var, rstd, nmr = (rows[:1, i * CH4:(i + 1) * CH4] for i in range(5))
                p.waits("vector", [t_sx, t_sq])
                p.op("vector", lambda e, mrow=mrow: e.tensor_scalar_mul(mrow, p_s1[:1, :], 1.0 / D))
                t_ps_read = p.op("vector", lambda e, e2r=e2r: e.tensor_scalar_mul(
                    e2r, p_s2[:1, :], 1.0 / D), inc="VE")
                p.op("vector", lambda e, var=var, mrow=mrow: e.tensor_mul(var, mrow, mrow))
                p.op("vector", lambda e, var=var, e2r=e2r: e.tensor_sub(var, e2r, var))
                t_vr = p.op("vector", lambda e, var=var: e.tensor_scalar_add(var, var, EPS), inc="VE")
                p.wait("scalar", t_vr)
                p.op("scalar", lambda e, rstd=rstd, var=var: e.activation(rstd, var, AF.Ln))
                t_ex = p.op("scalar", lambda e, rstd=rstd: e.activation(
                    rstd, rstd, AF.Exp, scale=-0.5), inc="SE")
                p.wait("vector", t_ex)
                t_nm = p.op("vector", lambda e, nmr=nmr, mrow=mrow, rstd=rstd: e.scalar_tensor_tensor(
                    nmr, mrow, -1.0, rstd, ALU.mult, ALU.mult), inc="VE")
                p.waits("tensor", [t_ex, t_nm, t_pb_read])
                tb1 = p.op("tensor", lambda e, rstd=rstd: e.matmul(
                    p_b1[:, :], onesrow[:1, :], rstd, start=True, stop=True), inc="TE")
                tb2 = p.op("tensor", lambda e, nmr=nmr: e.matmul(
                    p_b2[:, :], onesrow[:1, :], nmr, start=True, stop=True), inc="TE")
                p.waits("vector", [tb1, tb2])
                p.op("vector", lambda e: e.tensor_copy(rbs[:, :], p_b1[:, :]))
                t_pb_read = p.op("vector", lambda e: e.tensor_copy(mbs[:, :], p_b2[:, :]), inc="VE")
                p.wait("vector", t_store[1])
                tva = None
                for kt in range(32):
                    sl = x2f[:, kt * CH4:(kt + 1) * CH4]
                    dsl = xlnb[:, kt * CH4:(kt + 1) * CH4]
                    p.op("vector", lambda e, sl=sl: e.tensor_mul(tmpt[:, :], sl, rbs[:, :]))
                    tva = p.op("vector", lambda e, dsl=dsl: e.tensor_add(dsl, tmpt[:, :], mbs[:, :]), inc="VE")
                p.wait("scalar", tva)
                apo = bass.AP(s_xln2, mc * CH4, [[OWN, 128], [128 * OWN, 32], [1, CH4]])
                t_store[1] = p.dma("scalar", apo, xlnb[:, :], "ST")
            barrier()

        with ExitStack() as ph:
            xlp = ph.enter_context(nc.sbuf_tensor("xlp", [128, 32 * OWN], BF16))
            gones = ph.enter_context(nc.sbuf_tensor("sb_gones", [128, 32 * 32], BF16))
            giones = ph.enter_context(nc.sbuf_tensor("sb_giones", [128, 11 * 86], BF16))
            xg_sb = ph.enter_context(nc.sbuf_tensor("xg_sb", [32, OWN], F32))

            with ExitStack() as bh:
                qzb = Rot([bh.enter_context(nc.sbuf_tensor(f"qzb{i}", [32, 128], I32)) for i in range(2)])
                scb = Rot([bh.enter_context(nc.sbuf_tensor(f"scb{i}", [32, 1024], F32)) for i in range(2)])
                zpb = bh.enter_context(nc.sbuf_tensor("zpb", [32, 1024], I32))
                snb = bh.enter_context(nc.sbuf_tensor("snb", [32, 1024], F32))
                m2b = Rot([bh.enter_context(nc.sbuf_tensor(f"m2b{i}", [32, 1024], F32)) for i in range(2)])
                for which, qz_d, scp_d, m2_d in (("g", d_qzg, d_scg_p, s_m2g),
                                                 ("u", d_qzu, d_scu_p, s_m2u)):
                    for kb2 in range(11):
                        Mj = 128 if kb2 < 10 else 96
                        qb, warz = qzb.next()
                        p.wait("sync", warz)
                        tz = p.dma("sync", qb[:, 0:Mj],
                                   bass.AP(qz_d, kb2 * 128, [[1376, 32], [1, Mj]]), "J%d" % qzb.slot)
                        sb, wars = scb.next()
                        p.wait("sync", wars)
                        ts = p.dma("sync", sb[:, :],
                                   bass.AP(scp_d, kb2 * 1024, [[NT_I * 128, 32], [1, 1024]]), "M%d" % scb.slot)
                        tun = None
                        p.waits("vector", [tz, ts])
                        for t8 in range(8):
                            tun = p.op("vector", lambda e, qb=qb, t8=t8, Mj=Mj: e.tensor_scalar(
                                zpb[:, t8 * 128:t8 * 128 + Mj], qb[:, 0:Mj], 4 * t8, 15,
                                ALU.logical_shift_right, ALU.bitwise_and), inc="VE")
                        qzb.done(tun)
                        p.op("vector", lambda e, sb=sb: e.tensor_scalar_mul(snb[:, :], sb[:, :], -1.0))
                        mb, warm = m2b.next()
                        p.wait("vector", warm)
                        tm = p.op("vector", lambda e, mb=mb: e.scalar_tensor_tensor(
                            mb[:, :], zpb[:, :], 1.0, snb[:, :], ALU.add, ALU.mult), inc="VE")
                        scb.done(tm)
                        p.wait("scalar", tm)
                        tst = p.dma("scalar", bass.AP(m2_d, kb2 * 1024, [[NT_I * 128, 32], [1, 1024]]),
                                    mb[:, :], "ST")
                        m2b.done(tst)
                t_m2done = p.tok("ST")
            p.wait("sync", p.tok("VE"))

            p.dma("sync", gones[:, :], d_gones[:, :], "LD")
            tgi = p.dma("sync", giones[:, :], d_giones[:, :], "LD")
            tgo = tgi
            t_xlp = None
            for kbt in range(4):
                for s in range(8):
                    ti = kbt * 8 + s
                    apx = bass.AP(s_xln2, (1024 * kbt + s) * OWN, [[8 * OWN, 128], [1, OWN]])
                    t_xlp = p.dma("sync", xlp[:, ti * OWN:(ti + 1) * OWN], apx, "XL")

            qwb = Rot([ph.enter_context(nc.sbuf_tensor(f"qwb{i}", [128, 1024], I32)) for i in range(2)])
            w4 = ph.enter_context(nc.sbuf_tensor("w4", [128, 1024], I32))
            wbf = ph.enter_context(nc.sbuf_tensor("wbfg", [128, 32 * 1024], BF16))
            gsil = ph.enter_context(nc.sbuf_tensor("gsil", [128, 8 * OWN], F32))
            hbuf = Rot([ph.enter_context(nc.sbuf_tensor(f"hbuf{i}", [128, 512], BF16)) for i in range(2)])
            sxl = Rot([ph.enter_context(nc.sbuf_tensor(f"sxl{i}", [128, 1024], F32)) for i in range(2)])
            m2sl_bufs = [ph.enter_context(nc.sbuf_tensor(f"m2sl{i}", [32, 128], F32)) for i in range(4)]
            m2sl_tok = [None] * 4
            m2sl_i = [0]
            p_h_bufs = [ph.enter_context(nc.psum_tensor(f"p_h{i}", [128, 512], F32)) for i in range(6)]
            p_h_tok = [None] * 6
            p_h_i = [0]
            p_hg = ph.enter_context(nc.psum_tensor("p_hg", [86, OWN], F32))

            p.waits("tensor", [t_xlp, tgo])
            t_xg_mm = [None, None]
            for tc in range(2):
                psl = p_h_bufs[tc]
                for kt in range(32):
                    t_xg_mm[tc] = p.op("tensor", lambda e, psl=psl, kt=kt, tc=tc: e.matmul(
                        psl[0:32, :], gones[:, kt * 32:(kt + 1) * 32],
                        xlp[:, kt * OWN + tc * 512:kt * OWN + tc * 512 + 512],
                        start=(kt == 0), stop=(kt == 31)), inc="TE" if kt == 31 else None)
            p.waits("vector", t_xg_mm)
            t_xge = None
            for tc in range(2):
                t_xge = p.op("vector", lambda e, tc=tc, psl=p_h_bufs[tc]: e.tensor_copy(
                    xg_sb[:, tc * 512:(tc + 1) * 512], psl[0:32, :]), inc="VE")
            p_h_tok[0] = t_xge
            p_h_tok[1] = t_xge

            s2groups = ((0, 1, 2), (3, 4, 5), (6, 7))
            t_gsil_ev = {}
            t_gsil_free = {}
            t_chunkread = {}
            for kb2 in range(11):
                W = 1024 if kb2 < 10 else 768
                Mj = 128 if kb2 < 10 else 96
                for which in ("g", "u"):
                    qw_d = d_qwg if which == "g" else d_qwu
                    sx_d = d_sxg if which == "g" else d_sxu
                    m2_d = s_m2g if which == "g" else s_m2u
                    t_wready = {}
                    for kbt in range(4):
                        qb, warq = qwb.next()
                        p.wait("sync", warq)
                        apq = bass.AP(qw_d, (kbt * 128) * INTER + kb2 * 1024, [[INTER, 128], [1, W]])
                        tq = p.dma("sync", qb[:, 0:W], apq, "G%d" % qwb.slot)
                        sxb, warsx = sxl.next()
                        p.wait("sync", warsx)
                        tsx = p.dma("sync", sxb[:, 0:W],
                                    bass.AP(sx_d, kbt * INTER + kb2 * 1024, [[4 * INTER, 128], [1, W]]),
                                    "H%d" % sxl.slot)
                        tw = None
                        p.waits("vector", [tq, tsx, t_chunkread.get(kbt)])
                        for s in range(8):
                            p.op("vector", lambda e, qb=qb, s=s, W=W: e.tensor_scalar(
                                w4[:, 0:W], qb[:, 0:W], 4 * s, 15,
                                ALU.logical_shift_right, ALU.bitwise_and))
                            dst = wbf[:, (kbt * 8 + s) * 1024:(kbt * 8 + s) * 1024 + W]
                            tw = p.op("vector", lambda e, dst=dst, W=W, sxb=sxb: e.tensor_mul(
                                dst, w4[:, 0:W], sxb[:, 0:W]), inc="VE")
                        qwb.done(tw)
                        sxl.done(tw)
                        t_wready[kbt] = tw
                    for gi, s2grp in enumerate(s2groups):
                        pbs = {}
                        for s2 in s2grp:
                            for tc in range(2):
                                i6 = p_h_i[0] % 6
                                p_h_i[0] += 1
                                pbs[(s2, tc)] = (p_h_bufs[i6], i6)
                                p.wait("tensor", p_h_tok[i6])
                        t_ml = {}
                        msls = {}
                        for s2 in s2grp:
                            i4 = m2sl_i[0] % 4
                            m2sl_i[0] += 1
                            msl_b = m2sl_bufs[i4]
                            p.waits("sync", [m2sl_tok[i4],
                                             t_m2done if kb2 == 0 and which == "g" and gi == 0 and s2 == 0 else None])
                            tile_i = kb2 * 8 + s2
                            t_ml[s2] = (p.dma("sync", msl_b[:, 0:Mj],
                                              bass.AP(m2_d, tile_i * 128, [[NT_I * 128, 32], [1, Mj]]),
                                              "I%d" % i4), i4)
                            msls[s2] = msl_b
                        last_gi = gi == len(s2groups) - 1
                        for ktile in range(32):
                            kbt = ktile // 8
                            if ktile % 8 == 0:
                                p.waits("tensor", [t_wready[kbt]])
                            for si, s2 in enumerate(s2grp):
                                lh = wbf[:, ktile * 1024 + s2:ktile * 1024 + W:8]
                                for tc in range(2):
                                    inc = "TE" if (last_gi and ktile % 8 == 7
                                                   and si == len(s2grp) - 1 and tc == 1) else None
                                    t_ck = p.op("tensor", lambda e, psh=pbs[(s2, tc)][0], lh=lh,
                                                ktile=ktile, Mj=Mj, tc=tc: e.matmul(
                                        psh[0:Mj, :], lh,
                                        xlp[:, ktile * OWN + tc * 512:ktile * OWN + tc * 512 + 512],
                                        start=(ktile == 0), stop=False), inc=inc)
                                    if inc:
                                        t_chunkread[kbt] = t_ck
                        for s2 in s2grp:
                            tile_i = kb2 * 8 + s2
                            for tc in range(2):
                                psh, i6 = pbs[(s2, tc)]
                                p.wait("tensor", t_ml[s2][0])
                                tmm = p.op("tensor", lambda e, psh=psh, msl=msls[s2], Mj=Mj, tc=tc: e.matmul(
                                    psh[0:Mj, :], msl[:, 0:Mj], xg_sb[:, tc * 512:(tc + 1) * 512],
                                    start=False, stop=True), inc="TE")
                                if tc == 1:
                                    m2sl_tok[t_ml[s2][1]] = tmm
                                if which == "g":
                                    p.waits("scalar", [tmm, t_gsil_free.get((s2, tc))])
                                    dst = gsil[:, s2 * OWN + tc * 512:s2 * OWN + tc * 512 + 512]
                                    tev = p.op("scalar", lambda e, dst=dst, psh=psh, Mj=Mj: e.activation(
                                        dst[0:Mj, :], psh[0:Mj, :], AF.Silu), inc="SE")
                                    t_gsil_ev[(s2, tc)] = tev
                                    p_h_tok[i6] = tev
                                else:
                                    hb, warhb = hbuf.next()
                                    p.waits("vector", [tmm, t_gsil_ev[(s2, tc)], warhb])
                                    th = p.op("vector", lambda e, hb=hb, psh=psh, Mj=Mj, s2=s2, tc=tc: e.tensor_mul(
                                        hb[0:Mj, :], psh[0:Mj, :],
                                        gsil[0:Mj, s2 * OWN + tc * 512:s2 * OWN + tc * 512 + 512]), inc="VE")
                                    p_h_tok[i6] = th
                                    t_gsil_free[(s2, tc)] = th
                                    p.waits("tensor", [th, tgi if tile_i == 0 else None])
                                    glh = giones[:, kb2 * 86:(kb2 + 1) * 86]
                                    is_first = (kb2 == 0 and s2 == 0)
                                    is_last = (kb2 == 10 and s2 == 7)
                                    thg = p.op("tensor", lambda e, glh=glh, hb=hb, Mj=Mj, tc=tc,
                                               st=is_first, sp=is_last: e.matmul(
                                        p_hg[:, tc * 512:(tc + 1) * 512], glh[0:Mj, :], hb[0:Mj, :],
                                        start=st, stop=sp), inc="TE")
                                    p.wait("scalar", thg)
                                    aph = bass.AP(s_hT, (tile_i * 128) * OWN + tc * 512, [[OWN, Mj], [1, 512]])
                                    tst = p.dma("scalar", aph, hb[0:Mj, :], "ST")
                                    hbuf.done([tst, thg])
                                    if is_last and tc == 1:
                                        t_hg_last = thg
            p.wait("vector", t_hg_last)
            t_hge = p.op("vector", lambda e: e.tensor_copy(hg_sb[:, :], p_hg[:, :]), inc="VE")
            barrier()

        with ExitStack() as ph:
            m2nd = ph.enter_context(nc.sbuf_tensor("m2nd", [86, D], BF16))
            with ExitStack() as bh:
                scd_n = bh.enter_context(nc.sbuf_tensor("scd_n", [86, D], F32))
                qzd = bh.enter_context(nc.sbuf_tensor("sb_qzd", [86, 512], I32))
                zt6 = bh.enter_context(nc.sbuf_tensor("zt6", [86, D], I32))
                sn6 = bh.enter_context(nc.sbuf_tensor("sn6", [86, D], F32))
                tzd = p.dma("sync", qzd[:, :], d_qzd[:, :], "LD")
                tsd = p.dma("sync", scd_n[:, :], d_scd[:, :], "LD")
                p.waits("vector", [tsd])
                for t8 in range(8):
                    p.op("vector", lambda e, t8=t8: e.tensor_scalar(
                        zt6[:, t8:D:8], qzd[:, 0:512], 4 * t8, 15,
                        ALU.logical_shift_right, ALU.bitwise_and))
                p.op("vector", lambda e: e.tensor_scalar_mul(sn6[:, :], scd_n[:, :], -1.0))
                t_m2d = p.op("vector", lambda e: e.scalar_tensor_tensor(
                    m2nd[:, :], zt6[:, :], 1.0, sn6[:, :], ALU.add, ALU.mult), inc="VE")
            p.wait("sync", t_m2d)

            hT = ph.enter_context(nc.sbuf_tensor("hT", [128, NT_I * OWN], BF16))
            qwb6 = Rot([ph.enter_context(nc.sbuf_tensor(f"qwb6{i}", [128, 384], I32)) for i in range(2)])
            w46 = ph.enter_context(nc.sbuf_tensor("w46", [128, 384], I32))
            wdb = Rot([ph.enter_context(nc.sbuf_tensor(f"wdb{i}", [128, 384], BF16)) for i in range(2)])
            sx6 = Rot([ph.enter_context(nc.sbuf_tensor(f"sx6{i}", [128, 384], F32)) for i in range(2)])
            x2t = Rot([ph.enter_context(nc.sbuf_tensor(f"x2t{i}", [128, 512], F32)) for i in range(2)])
            ot = Rot([ph.enter_context(nc.sbuf_tensor(f"ot{i}", [128, 512], F32)) for i in range(2)])
            p_o = [ph.enter_context(nc.psum_tensor(f"p_o{i}", [128, 512], F32)) for i in range(6)]

            t_hl = None
            for ti in range(NT_I):
                rows_n = 128 if ti < 80 else 96
                aph = bass.AP(s_hT, (ti * 128) * OWN, [[OWN, rows_n], [1, OWN]])
                t_hl = p.dma("sync", hT[0:rows_n, ti * OWN:(ti + 1) * OWN], aph, "HL")

            db_list = [(0, 3), (3, 3), (6, 3), (9, 3), (12, 3), (15, 3),
                       (18, 3), (21, 3), (24, 3), (27, 3), (30, 2)]
            t_ofree = {}
            for dbi, (dm0, ndm) in enumerate(db_list):
                col0 = dm0 * 128
                ncol = ndm * 128
                ic = 0
                t_dmm = None
                for kb2 in range(11):
                    Mj = 128 if kb2 < 10 else 96
                    qb, warq = qwb6.next()
                    p.wait("sync", warq)
                    apq = bass.AP(d_qwd, (kb2 * 128) * D + col0, [[D, Mj], [1, ncol]])
                    tq = p.dma("sync", qb[0:Mj, 0:ncol], apq, "G%d" % qwb6.slot)
                    sxb, warsc = sx6.next()
                    p.wait("sync", warsc)
                    tsc = p.dma("sync", sxb[:, 0:ncol],
                                bass.AP(d_sxd, kb2 * D + col0, [[11 * D, 128], [1, ncol]]), "H%d" % sx6.slot)
                    tw = None
                    for s2 in range(8):
                        wd, warw = wdb.next()
                        p.waits("vector", [warw] + ([tq, tsc] if s2 == 0 else []))
                        p.op("vector", lambda e, qb=qb, s2=s2, Mj=Mj, ncol=ncol: e.tensor_scalar(
                            w46[0:Mj, 0:ncol], qb[0:Mj, 0:ncol], 4 * s2, 15,
                            ALU.logical_shift_right, ALU.bitwise_and))
                        tw = p.op("vector", lambda e, wd=wd, sxb=sxb, Mj=Mj, ncol=ncol: e.tensor_mul(
                            wd[0:Mj, 0:ncol], w46[0:Mj, 0:ncol], sxb[0:Mj, 0:ncol]), inc="VE")
                        tile_i = kb2 * 8 + s2
                        p.waits("tensor", [tw, t_hl if dbi == 0 and kb2 == 0 and s2 == 0 else None]
                                + ([t_ofree.get((dbi - 1, dmi2, tc2))
                                    for dmi2 in range(3) for tc2 in range(2)]
                                   if ic == 0 else []))
                        for dmi in range(ndm):
                            for tc in range(2):
                                t_dmm = p.op("tensor", lambda e, po=p_o[dmi * 2 + tc], wd=wd, dmi=dmi,
                                             tc=tc, Mj=Mj, tile_i=tile_i, ic=ic: e.matmul(
                                    po[:, :], wd[0:Mj, dmi * 128:(dmi + 1) * 128],
                                    hT[0:Mj, tile_i * OWN + tc * 512:tile_i * OWN + tc * 512 + 512],
                                    start=(ic == 0), stop=False),
                                    inc="TE" if (dmi == ndm - 1 and tc == 1) else None)
                        wdb.done(t_dmm)
                        ic += 1
                    qwb6.done(tw)
                    sx6.done(tw)
                p.waits("tensor", [t_m2d, t_hge])
                tlast = {}
                for dmi in range(ndm):
                    for tc in range(2):
                        msl = m2nd[:, col0 + dmi * 128:col0 + (dmi + 1) * 128]
                        tlast[(dmi, tc)] = p.op("tensor", lambda e, po=p_o[dmi * 2 + tc], msl=msl, tc=tc: e.matmul(
                            po[:, :], msl, hg_sb[:, tc * 512:(tc + 1) * 512],
                            start=False, stop=True), inc="TE")
                for dmi in range(ndm):
                    for tc in range(2):
                        xb, warx = x2t.next()
                        p.wait("sync", warx)
                        apx = bass.AP(s_x2, (col0 + dmi * 128) * OWN + tc * 512, [[OWN, 128], [1, 512]])
                        tx = p.dma("sync", xb[:, :], apx, "E%d" % x2t.slot)
                        ob, waro = ot.next()
                        p.waits("vector", [tx, tlast[(dmi, tc)], waro])
                        tno = p.op("vector", lambda e, ob=ob, xb=xb, po=p_o[dmi * 2 + tc]: e.tensor_add(
                            ob[:, :], po[:, :], xb[:, :]), inc="VE")
                        x2t.done(tno)
                        t_ofree[(dbi, dmi, tc)] = tno
                        p.wait("scalar", tno)
                        apo = bass.AP(d_out, (col0 + dmi * 128) * OWN + tc * 512, [[OWN, 128], [1, 512]])
                        tst = p.dma("scalar", apo, ob[:, :], "ST")
                        ot.done(tst)
            p.wait("sync", p.tok("ST"))

        with nc.Block() as block:
            @block.sync
            def _(e):
                for f in p.q["sync"]:
                    f(e)

            @block.tensor
            def _(e):
                for f in p.q["tensor"]:
                    f(e)

            @block.vector
            def _(e):
                for f in p.q["vector"]:
                    f(e)

            @block.scalar
            def _(e):
                for f in p.q["scalar"]:
                    f(e)

            @block.gpsimd
            def _(e):
                for f in p.q["gpsimd"]:
                    f(e)

    return nc


def _stage_inputs(inputs):
    f32 = np.float32
    x = np.asarray(inputs["L0_attn_x"], f32)
    assert np.array_equal(np.asarray(inputs["L0_ffn_gate_g_idx"]),
                          (np.arange(4096) // 128).astype(np.int32))
    assert np.array_equal(np.asarray(inputs["L0_ffn_up_g_idx"]),
                          (np.arange(4096) // 128).astype(np.int32))
    assert np.array_equal(np.asarray(inputs["L0_ffn_down_g_idx"]),
                          (np.arange(11008) // 128).astype(np.int32))

    com = {}
    com["wqT"] = np.ascontiguousarray(np.asarray(inputs["L0_attn_Wq"], f32).T)
    com["wkT"] = np.ascontiguousarray(np.asarray(inputs["L0_attn_Wk"], f32).T)
    com["wvT"] = np.ascontiguousarray(np.asarray(inputs["L0_attn_Wv"], f32).T)
    com["bq"] = np.ascontiguousarray(np.asarray(inputs["L0_attn_bq"], f32).reshape(32, 128).T)
    com["bk"] = np.ascontiguousarray(np.asarray(inputs["L0_attn_bk"], f32).reshape(32, 128).T)
    com["bv"] = np.asarray(inputs["L0_attn_bv"], f32).reshape(1, 4096).astype(bfloat16)
    com["bw"] = np.asarray(inputs["L0_attn_bias_weight"], f32).reshape(1, 48)
    kb = _akron_bases()
    kbs = np.zeros((128, 4 * 2 * 256), dtype=f32)
    for i in range(4):
        for kt2 in range(2):
            kbs[:, (i * 2 + kt2) * 256:(i * 2 + kt2 + 1) * 256] = kb[i][kt2 * 128:(kt2 + 1) * 128, :]
    com["kbasis"] = kbs
    com["onesrow"] = np.ones((1, 128), f32)
    com["onesrow_bf"] = np.ones((1, 128), bfloat16)
    com["onescol_bf"] = np.ones((128, 1), bfloat16)
    com["onescol_f"] = np.ones((128, 1), f32)
    gn = np.zeros((128, 32 * 32), bfloat16)
    for kt in range(32):
        kbt = kt // 8
        for j in range(128):
            gn[j, kt * 32 + kbt * 8 + j // 16] = 1.0
    com["gones"] = gn
    gi = np.zeros((128, 11 * 86), bfloat16)
    for kb2 in range(11):
        maxp = 128 if kb2 < 10 else 96
        for ps in range(maxp):
            gi[ps, kb2 * 86 + 8 * kb2 + ps // 16] = 1.0
    com["giones"] = gi
    com["qwg"] = np.ascontiguousarray(np.asarray(inputs["L0_ffn_gate_qweight"], np.int32))
    com["qwu"] = np.ascontiguousarray(np.asarray(inputs["L0_ffn_up_qweight"], np.int32))
    com["qwd"] = np.ascontiguousarray(np.asarray(inputs["L0_ffn_down_qweight"], np.int32))
    scg = np.asarray(inputs["L0_ffn_gate_scales"], f32)
    scu = np.asarray(inputs["L0_ffn_up_scales"], f32)
    scd = np.asarray(inputs["L0_ffn_down_scales"], f32)
    com["scd"] = scd
    prow = np.arange(128) // 16
    sgt = np.zeros((128, 4 * INTER), f32)
    sut = np.zeros((128, 4 * INTER), f32)
    for kbt in range(4):
        sgt[:, kbt * INTER:(kbt + 1) * INTER] = scg[kbt * 8 + prow, :]
        sut[:, kbt * INTER:(kbt + 1) * INTER] = scu[kbt * 8 + prow, :]
    com["sxg"], com["sxu"] = sgt, sut
    sdt = np.zeros((128, 11 * D), f32)
    for kb2 in range(11):
        ng = 8 if kb2 < 10 else 6
        rows = np.minimum(prow, ng - 1)
        sdt[:, kb2 * D:(kb2 + 1) * D] = scd[kb2 * 8 + rows, :]
    com["sxd"] = sdt
    cols = np.zeros(NT_I * 128, dtype=np.int64)
    valid = np.zeros(NT_I * 128, dtype=bool)
    for t in range(NT_I):
        kb2, s2 = divmod(t, 8)
        nj = 128 if kb2 < 10 else 96
        j = np.arange(nj)
        cols[t * 128:t * 128 + nj] = kb2 * 1024 + 8 * j + s2
        valid[t * 128:t * 128 + nj] = True
    scg_p = np.zeros((32, NT_I * 128), f32)
    scu_p = np.zeros((32, NT_I * 128), f32)
    scg_p[:, valid] = scg[:, cols[valid]]
    scu_p[:, valid] = scu[:, cols[valid]]
    com["scg_p"], com["scu_p"] = scg_p, scu_p
    com["qzg"] = np.ascontiguousarray(np.asarray(inputs["L0_ffn_gate_qzeros"], np.int32))
    com["qzu"] = np.ascontiguousarray(np.asarray(inputs["L0_ffn_up_qzeros"], np.int32))
    com["qzd"] = np.ascontiguousarray(np.asarray(inputs["L0_ffn_down_qzeros"], np.int32))

    xT = [np.ascontiguousarray(x[b].T) for b in range(2)]
    in_maps = []
    for core in range(8):
        b, q = core // 4, core % 4
        m = dict(com)
        if q == 0:
            m["xT"] = xT[b]
        else:
            m["xT"] = np.ascontiguousarray(
                np.concatenate([xT[b][:, q * 1024:], xT[b][:, :q * 1024]], axis=1))
        in_maps.append(m)
    return in_maps


def kernel(**inputs):
    in_maps = _stage_inputs(inputs)
    nc = build_program()
    res = run_bass_kernel_spmd(nc, in_maps, core_ids=list(range(8)))
    out = np.empty((2, 4096, 4096), dtype=np.float32)
    for core in range(8):
        b, q = core // 4, core % 4
        out[b, q * 1024:(q + 1) * 1024, :] = np.asarray(res.results[core]["out"]).T
    return out
